# version 07
# baseline (speedup 1.0000x reference)
import numpy as np
import ml_dtypes

import concourse.bass as bass
import concourse.mybir as mybir
from concourse.bass_utils import run_bass_kernel_spmd

F32 = mybir.dt.float32
BF16 = mybir.dt.bfloat16
U8 = mybir.dt.uint8
AF = mybir.ActivationFunctionType
ALU = mybir.AluOpType
AX = mybir.AxisListType

T = 2048
D = 1024
DC = 8
NSL = 4
SL = 512
NTC = 16
NEG_BIG = -1.0e9

ENGINES = ("pe", "act", "dve", "pool", "sp")
SAME_ENGINE_SYNC = {"act", "dve", "pool", "sp"}

PSUM_F32 = 4096


class Tile:
    __slots__ = ("name", "ap_full", "rng", "space", "lw", "readers", "prog")

    def __init__(self, prog, name, ap_full, rng, space):
        self.prog = prog
        self.name = name
        self.ap_full = ap_full
        self.rng = rng
        self.space = space
        self.lw = None
        self.readers = []

    def __getitem__(self, idx):
        return self.ap_full[idx]

    @property
    def ap(self):
        return self.ap_full


class Op:
    __slots__ = ("engine", "idx", "fn", "clock", "inc", "waits", "tag")

    def __init__(self, engine, idx, fn, clock, tag=""):
        self.engine = engine
        self.idx = idx
        self.fn = fn
        self.clock = clock
        self.inc = False
        self.waits = []
        self.tag = tag


class DmaEv:
    __slots__ = ("key", "count", "clock", "bulk")

    def __init__(self, key, count, clock, bulk):
        self.key = key
        self.count = count
        self.clock = clock
        self.bulk = bulk


class Prog:
    def __init__(self):
        self.ops = {e: [] for e in ENGINES}
        self.clock = {e: {} for e in ENGINES}
        self.dma_count = {}
        self.sem_keys = []
        self.sb_free = None
        self.ps_free = [(0, PSUM_F32 * 4)]
        self.sb_hist = []
        self.ps_hist = []
        self.live = {}
        self.nwaits = 0
        self.phase = 0
        self.nc = None
        self.sb = None
        self.ps = None

    def _alloc(self, free, nbytes, align):
        for i, (a, b) in enumerate(free):
            a2 = (a + align - 1) // align * align
            if b - a2 >= nbytes:
                if a2 > a:
                    free[i] = (a, a2)
                    free.insert(i + 1, (a2 + nbytes, b))
                else:
                    free[i] = (a + nbytes, b)
                if free[i][0] >= free[i][1]:
                    free.pop(i)
                return a2
        raise MemoryError(f"arena OOM for {getattr(self, chr(95)+chr(108)+chr(97)+chr(115)+chr(116)+chr(95)+chr(97)+chr(108)+chr(108)+chr(111)+chr(99), chr(63))}: need {nbytes}, free={free}, live={ {k: v for k, v in sorted(getattr(self, chr(108)+chr(105)+chr(118)+chr(101), {}).items()) if v} }")

    def _collect_hist(self, hist, rng):
        evs = []
        keep = []
        for r, e in hist:
            if r[0] < rng[1] and rng[0] < r[1]:
                evs.extend(e)
            else:
                keep.append((r, e))
        hist[:] = keep
        return evs

    def alloc(self, name, shape, dtype, align=64):
        esz = 2 if dtype == BF16 else 4
        nbytes = shape[-1] * esz
        self._last_alloc = name
        off = self._alloc(self.sb_free, nbytes, align)
        view = self.sb.bitcast(dtype)
        e0 = off // esz
        ap = view[0:shape[0], e0:e0 + shape[-1]]
        t = Tile(self, name, ap, (off, off + nbytes), "sb")
        self.live[name] = self.live.get(name, 0) + nbytes
        evs = self._collect_hist(self.sb_hist, t.rng)
        if evs:
            t.readers = evs
        return t

    def palloc(self, name, shape):
        nbytes = (shape[-1] * 4 + 2047) // 2048 * 2048
        off = self._alloc(self.ps_free, nbytes, 2048)
        e0 = off // 4
        ap = self.ps[0:shape[0], e0:e0 + shape[-1]]
        t = Tile(self, name, ap, (off, off + nbytes), "ps")
        evs = self._collect_hist(self.ps_hist, t.rng)
        if evs:
            t.readers = evs
        return t

    def free(self, t):
        if isinstance(t, (list, tuple)):
            for y in t:
                self.free(y)
            return
        if isinstance(t, dict):
            for y in t.values():
                self.free(y)
            return
        free, hist = (self.sb_free, self.sb_hist) if t.space == "sb" else (self.ps_free, self.ps_hist)
        if t.space == "sb":
            self.live[t.name] = self.live.get(t.name, 0) - (t.rng[1] - t.rng[0])
        evs = list(t.readers)
        if t.lw is not None:
            evs.append(t.lw)
        if evs:
            hist.append((t.rng, evs))
        a, b = t.rng
        free.append((a, b))
        free.sort()
        merged = []
        for r in free:
            if merged and r[0] <= merged[-1][1]:
                merged[-1] = (merged[-1][0], max(merged[-1][1], r[1]))
            else:
                merged.append(r)
        free[:] = merged

    def dram_tile(self, name, ap):
        return Tile(self, name, ap, None, "dram")

    @staticmethod
    def bcast_ap(tile, col_off, groups, reps):
        base = tile.ap_full
        pat = [list(base.ap[0]), [1, groups], [0, reps]]
        return bass.AP(base.tensor, base.offset + col_off, pat)

    def _need(self, myeng, myclock, ev):
        if isinstance(ev, DmaEv):
            if myclock.get("sem:" + ev.key, 0) >= ev.count:
                return None
            return ev
        if ev.engine == myeng == "pe":
            return None
        if myclock.get(ev.engine, -1) >= ev.idx:
            return None
        return ev

    def op(self, engine, fn, reads=(), writes=(), tag="", register=True):
        myclock = self.clock[engine]
        idx = len(self.ops[engine])
        needed = []
        seen = set()

        def add(ev):
            r = self._need(engine, myclock, ev)
            if r is not None and id(r) not in seen:
                seen.add(id(r))
                needed.append(r)

        for t in reads:
            if t.lw is not None:
                add(t.lw)
        for t in writes:
            if t.lw is not None:
                add(t.lw)
            for r in t.readers:
                add(r)
        if len(needed) > 1:
            best = {}
            for ev in needed:
                k = ("s", ev.key) if isinstance(ev, DmaEv) else ("e", ev.engine)
                cur = best.get(k)
                v = ev.count if isinstance(ev, DmaEv) else ev.idx
                if cur is None or v > cur[0]:
                    best[k] = (v, ev)
            needed = [v[1] for v in best.values()]
        newclock = dict(myclock)
        for ev in needed:
            for k, v in ev.clock.items():
                if newclock.get(k, -1) < v:
                    newclock[k] = v
            if isinstance(ev, DmaEv):
                k = "sem:" + ev.key
                newclock[k] = max(newclock.get(k, 0), ev.count)
            else:
                ev.inc = True
                newclock[ev.engine] = max(newclock.get(ev.engine, -1), ev.idx)
        o = Op(engine, idx, fn, newclock, tag)
        o.waits = needed
        self.nwaits += len(needed)
        self.ops[engine].append(o)
        self.clock[engine] = newclock
        if register:
            for t in reads:
                t.readers.append(o)
            for t in writes:
                t.lw = o
                t.readers = []
        return o

    def dma(self, queue, key, dst, src, dst_ap=None, src_ap=None, bulk=True, tag=""):
        if key not in self.dma_count:
            self.dma_count[key] = 0
            self.sem_keys.append(key)
        self.dma_count[key] += 1
        cnt = self.dma_count[key] * 16
        d_ap = dst_ap if dst_ap is not None else dst.ap
        s_ap = src_ap if src_ap is not None else (src.ap if isinstance(src, Tile) else src)

        def fn(eng, _sems, _k=key, _d=d_ap, _s=s_ap):
            return eng.dma_start(_d, _s).then_inc(_sems[_k], 16)

        fn._dma = True
        reads = [src] if isinstance(src, Tile) else []
        writes = [dst]
        o = self.op(queue, fn, reads=reads, writes=writes, tag=tag or f"dma:{key}",
                    register=False)
        ev = DmaEv(key, cnt, o.clock, bulk)
        dst.lw = ev
        dst.readers = []
        if isinstance(src, Tile):
            src.readers.append(ev)
        return ev

    def emit(self, es, nc):
        sems = {}
        for e in ENGINES:
            sems[e] = es.enter_context(nc.semaphore(f"sem_{e}"))
        for k in self.sem_keys:
            sems[k] = es.enter_context(nc.semaphore(f"d_{k}"))
        incno = {}
        for e in ENGINES:
            c = 0
            nos = []
            for o in self.ops[e]:
                if o.inc:
                    c += 1
                nos.append(c)
            incno[e] = nos

        final = {k: v * 16 for k, v in self.dma_count.items()}

        def run_engine(eng_obj, ename):
            last_wait = {}
            for o in self.ops[ename]:
                for ev in o.waits:
                    if isinstance(ev, DmaEv):
                        sname = ev.key
                        val = final[ev.key] if ev.bulk else ev.count
                    else:
                        sname = ev.engine
                        val = incno[ev.engine][ev.idx]
                    if last_wait.get(sname, -1) >= val:
                        continue
                    last_wait[sname] = val
                    eng_obj.wait_ge(sems[sname], val)
                inst = o.fn(eng_obj, sems)
                if o.inc and not getattr(o.fn, "_dma", False):
                    inst.then_inc(sems[ename], 1)

        def run_sp(s):
            run_engine(s, "sp")
            for k in self.sem_keys:
                s.wait_ge(sems[k], final[k])

        block = es.enter_context(nc.Block())
        block.tensor(lambda t: run_engine(t, "pe"))
        block.scalar(lambda s: run_engine(s, "act"))
        block.vector(lambda v: run_engine(v, "dve"))
        block.gpsimd(lambda g: run_engine(g, "pool"))
        block.sync(run_sp)

    def mm(self, out_ap, lhsT_ap, rhs_ap, start, stop, reads, writes, tag=""):
        return self.op("pe", lambda e, _s, _o=out_ap, _l=lhsT_ap, _r=rhs_ap, _st=start, _sp=stop:
                       e.matmul(_o, _l, _r, start=_st, stop=_sp), reads=reads, writes=writes, tag=tag)

    def transp(self, out_ap, in_ap, idn_ap, reads, writes, tag="tr"):
        return self.op("pe", lambda e, _s, _o=out_ap, _i=in_ap, _d=idn_ap: e.transpose(_o, _i, _d),
                       reads=reads, writes=writes, tag=tag)

    def act(self, out_ap, in_ap, func, reads, writes, bias=0.0, scale=1.0, tag=""):
        return self.op("act", lambda e, _s, _o=out_ap, _i=in_ap, _f=func, _b=bias, _c=scale:
                       e.activation(_o, _i, _f, bias=_b, scale=_c), reads=reads, writes=writes, tag=tag)

    def tt(self, engine, out_ap, a_ap, b_ap, alu, reads, writes, tag=""):
        return self.op(engine, lambda e, _s, _o=out_ap, _a=a_ap, _b=b_ap, _u=alu:
                       e.tensor_tensor(_o, _a, _b, _u), reads=reads, writes=writes, tag=tag)

    def ts(self, engine, out_ap, in_ap, s1, s2, op0, op1, reads, writes, tag=""):
        if s2 is None:
            return self.op(engine, lambda e, _s, _o=out_ap, _i=in_ap, _1=s1, _p0=op0:
                           e.tensor_scalar(_o, _i, _1, None, _p0), reads=reads, writes=writes, tag=tag)
        return self.op(engine, lambda e, _s, _o=out_ap, _i=in_ap, _1=s1, _2=s2, _p0=op0, _p1=op1:
                       e.tensor_scalar(_o, _i, _1, _2, _p0, _p1), reads=reads, writes=writes, tag=tag)

    def stt(self, engine, out_ap, in0_ap, scalar, in1_ap, op0, op1, reads, writes, tag=""):
        return self.op(engine, lambda e, _s, _o=out_ap, _i=in0_ap, _c=scalar, _j=in1_ap, _p0=op0, _p1=op1:
                       e.scalar_tensor_tensor(_o, _i, _c, _j, _p0, _p1), reads=reads, writes=writes, tag=tag)

    def red(self, out_ap, in_ap, alu, reads, writes, negate=False, tag=""):
        return self.op("dve", lambda e, _s, _o=out_ap, _i=in_ap, _u=alu, _n=negate:
                       e.tensor_reduce(_o, _i, AX.X, _u, negate=_n), reads=reads, writes=writes, tag=tag)

    def recip(self, out_ap, in_ap, reads, writes, tag="recip"):
        return self.op("dve", lambda e, _s, _o=out_ap, _i=in_ap: e.reciprocal(_o, _i),
                       reads=reads, writes=writes, tag=tag)

    def vmax8(self, out_ap, in_ap, reads, writes, tag="max8"):
        return self.op("dve", lambda e, _s, _o=out_ap, _i=in_ap: e.max(out=_o, in_=_i),
                       reads=reads, writes=writes, tag=tag)

    def copy(self, engine, out_ap, in_ap, reads, writes, tag="cp"):
        return self.op(engine, lambda e, _s, _o=out_ap, _i=in_ap: e.tensor_copy(_o, _i),
                       reads=reads, writes=writes, tag=tag)

    def memset(self, engine, ap, val, writes, tag="memset"):
        return self.op(engine, lambda e, _s, _a=ap, _v=val: e.memset(_a, _v),
                       reads=(), writes=writes, tag=tag)


class PsumPool:
    def __init__(self, p, name, shape, n=2):
        self.p = p
        self.tiles = [p.palloc(f"{name}{i}", shape) for i in range(n)]
        self.i = 0

    def get(self):
        t = self.tiles[self.i % len(self.tiles)]
        self.i += 1
        return t

    def free(self):
        self.p.free(self.tiles)



def build_nc(parts):
    from contextlib import ExitStack
    nc = bass.Bass("TRN2", target_bir_lowering=False)
    es = ExitStack()

    di = {}

    def inp(name, shape):
        di[name] = nc.dram_tensor(name, list(shape), F32, kind="ExternalInput")
        return di[name]

    XIN = inp("xin", [D, T])
    OUT = nc.dram_tensor("out", [D, T], F32, kind="ExternalOutput")
    XSPILL = nc.dram_tensor("xspill", [D, T], F32, kind="Internal")

    inp("c_misc", [128, 8])
    inp("c_onesrow", [1, 640])
    inp("c_onesrow_bf", [1, 640])
    inp("c_idn", [128, 128])
    inp("c_sel4", [4, 512])
    inp("c_sel32", [32, 4096])
    inp("c_ones4", [4, 128])
    inp("c_swap", [128, 128])
    inp("c_cos", [128, T])
    inp("c_sin", [128, T])
    inp("c_dmask", [128, 2048])

    inp("wq", [D, D]); inp("wk", [D, D]); inp("wv", [D, D]); inp("wo", [D, D])
    inp("Wg", [D, 32])
    inp("W1", [32, D, 512]); inp("W3", [32, D, 512]); inp("W2", [32, 512, D])
    inp("Ws1", [D, D]); inp("Ws3", [D, D]); inp("Ws2", [D, D])
    inp("waT", [D, 3 * D]); inp("ba", [1, 3 * D]); inp("wpT", [D, D]); inp("bp", [1, D])
    inp("rw", [4, D, 4]); inp("rb", [1, 16]); inp("rb16", [1, 256])
    inp("gcols", [128, 128]); inp("bcols", [128, 128]); inp("b1cols", [128, 512])
    inp("w1", [4, 4, D, 4096]); inp("w2", [4, 4, 4096, D])
    inp("b2l", [4, 4096])

    p = Prog()
    p.nc = nc
    arena_bytes = (nc.sbuf_bytes_remaining - 1024) // 64 * 64
    p.sb_free = [(0, arena_bytes)]
    p.sb = es.enter_context(nc.sbuf_tensor("arena", [128, arena_bytes], U8))
    p.ps = es.enter_context(nc.psum_tensor("psarena", [128, PSUM_F32], F32))

    CM = p.alloc("c_misc", [128, 8], F32)
    ONESR = p.alloc("c_onesrow", [1, 128], F32)
    ONESRB = p.alloc("c_onesrow_bf", [1, 640], BF16)
    IDN = p.alloc("c_idn", [128, 128], F32)
    SEL4 = p.alloc("c_sel4", [4, 512], BF16)
    ONES4 = p.alloc("c_ones4", [4, 128], BF16)
    GCOLS = p.alloc("gcols", [128, 128], F32)
    BCOLS = p.alloc("bcols", [128, 128], F32)
    B1COLS = p.alloc("b1cols", [128, 512], F32)
    RWB = p.alloc("rw", [128, 128], BF16)
    RB = p.alloc("rb", [1, 16], F32)
    RB16 = p.alloc("rb16", [1, 256], F32)
    B2L = p.alloc("b2l", [4, 4096], BF16)
    WGB = p.alloc("wg", [128, 256], BF16)

    p.dma("sp", "cst", ONESR, di["c_onesrow"].ap()[:, 0:128])
    p.dma("sp", "cst", RB16, di["rb16"].ap())
    for tile, name in ((CM, "c_misc"), (IDN, "c_idn"),
                       (GCOLS, "gcols"), (BCOLS, "bcols"), (B1COLS, "b1cols"), (RB, "rb")):
        p.dma("sp", "cst", tile, di[name].ap())
    for tile, name in ((ONESRB, "c_onesrow_bf"), (SEL4, "c_sel4"), (ONES4, "c_ones4"), (B2L, "b2l")):
        p.dma("pool", "cstb", tile, di[name].ap())
    p.dma("pool", "cstb", RWB, di["rw"].ap().rearrange("l (c p) e -> p l c e", p=128),
          dst_ap=RWB.ap.rearrange("p (l c e) -> p l c e", l=4, c=8))
    p.dma("pool", "cstb", WGB, di["Wg"].ap().rearrange("(c p) e -> p c e", p=128),
          dst_ap=WGB.ap.rearrange("p (c e) -> p c e", c=8))

    ONES_COL = CM[:, 0:1]
    EPS_S = CM[0:1, 1:2]
    ONESROW128 = ONESR[0:1, 0:128]
    ONESROW512B = ONESRB[0:1, 128:640]
    ONESROW128B = ONESRB[0:1, 0:128]

    x = [p.alloc(f"x{c}", [128, T], F32) for c in range(DC)]
    for c in range(DC):
        p.dma("sp", "xld", x[c], XIN.ap()[128 * c:128 * (c + 1), :])

    xsp = [p.dram_tile(f"xsp{c}", XSPILL.ap()[128 * c:128 * (c + 1), :]) for c in range(DC)]


    def emit_ln(src, dst_dtype):
        dst = [p.alloc(f"ln{i}", [128, T], dst_dtype) for i in range(DC)]
        sq = [p.alloc("lnsq0", [128, T], F32)]
        S1 = p.palloc("lnS1", [1, T])
        S2 = p.palloc("lnS2", [1, T])
        for c in range(DC):
            p.act(sq[0].ap, src[c].ap, AF.Square, reads=[src[c]], writes=[sq[0]], tag="ln_sq")
            for sl in range(NSL):
                p.mm(S1[0:1, SL * sl:SL * (sl + 1)], ONES_COL, src[c][:, SL * sl:SL * (sl + 1)],
                     start=(c == 0), stop=(c == DC - 1), reads=[CM, src[c]], writes=[S1], tag="ln_s1")
                p.mm(S2[0:1, SL * sl:SL * (sl + 1)], ONES_COL, sq[0][:, SL * sl:SL * (sl + 1)],
                     start=(c == 0), stop=(c == DC - 1), reads=[CM, sq[0]], writes=[S2], tag="ln_s2")
        rows = p.alloc("lnrows", [1, 3 * T], F32)
        m_row = rows[0:1, 0:T]
        e_row = rows[0:1, T:2 * T]
        w_row = rows[0:1, 2 * T:3 * T]
        r_row = w_row
        p.ts("dve", m_row, S1[0:1, :], 1.0 / D, None, ALU.mult, None, reads=[S1], writes=[rows], tag="ln_m")
        p.ts("dve", e_row, S2[0:1, :], 1.0 / D, None, ALU.mult, None, reads=[S2], writes=[rows], tag="ln_e")
        p.free(S1); p.free(S2)
        p.tt("dve", w_row, m_row, m_row, ALU.mult, reads=[rows], writes=[rows], tag="ln_mm")
        p.tt("dve", w_row, e_row, w_row, ALU.subtract, reads=[rows], writes=[rows], tag="ln_var")
        p.act(e_row, w_row, AF.Ln, reads=[rows, CM], writes=[rows], bias=EPS_S, tag="ln_ln")
        p.act(w_row, e_row, AF.Exp, reads=[rows], writes=[rows], scale=-0.5, tag="ln_rstd")
        p.stt("dve", m_row, m_row, -1.0, r_row, ALU.mult, ALU.mult, reads=[rows], writes=[rows], tag="ln_b")
        A = p.alloc("lnA", [128, T], F32)
        Bb = p.alloc("lnB", [128, T], F32)
        PA = p.palloc("lnPA", [128, T])
        for sl in range(NSL):
            p.mm(PA[:, SL * sl:SL * (sl + 1)], ONESROW128, r_row[0:1, SL * sl:SL * (sl + 1)],
                 start=True, stop=True, reads=[ONESR, rows], writes=[PA], tag="ln_bcA")
        p.copy("dve", A.ap, PA.ap, reads=[PA], writes=[A], tag="ln_evA")
        for sl in range(NSL):
            p.mm(PA[:, SL * sl:SL * (sl + 1)], ONESROW128, m_row[0:1, SL * sl:SL * (sl + 1)],
                 start=True, stop=True, reads=[ONESR, rows], writes=[PA], tag="ln_bcB")
        p.copy("dve", Bb.ap, PA.ap, reads=[PA], writes=[Bb], tag="ln_evB")
        p.free(PA)
        for c in range(DC):
            u = sq[0]
            p.tt("dve", u.ap, src[c].ap, A.ap, ALU.mult, reads=[src[c], A], writes=[u], tag="ln_u")
            p.tt("pool", dst[c].ap, u.ap, Bb.ap, ALU.add, reads=[u, Bb], writes=[dst[c]], tag="ln_o")
        p.free(rows); p.free(A); p.free(Bb); p.free(sq)
        return dst

    def emit_l0_ffn():
        p.phase += 1
        ph = p.phase
        xln = emit_ln(x, BF16)
        SEL32 = p.alloc("sel32", [32, 4096], BF16)
        p.dma("pool", f"cb{ph}", SEL32, di["c_sel32"].ap())
        GFM = p.alloc("gfm", [32, T], BF16)
        ex = p.alloc("rt_ex", [128, 512], F32)
        ex2 = p.alloc("rt_ex2", [128, 512], F32)
        exp_ = p.alloc("rt_exp", [128, 512], F32)
        scr = p.alloc("rt_nm", [128, 96], F32)
        PL = p.palloc("rt_ps", [128, 512])
        n = 0
        for tc in range(NTC):
            for c in range(DC):
                n += 1
                p.mm(PL[:, 32 * tc:32 * (tc + 1)], xln[c][:, 128 * tc:128 * (tc + 1)],
                     WGB[:, 32 * c:32 * (c + 1)],
                     start=(n == 1), stop=(n == NTC * DC), reads=[xln[c], WGB], writes=[PL], tag="rt_mm")
        PLv = PL.ap.rearrange("p (g e) -> p g e", e=32)
        nm_b = Prog.bcast_ap(scr, 0, NTC, 32)
        p.red(scr[:, 0:16], PLv, ALU.max, reads=[PL], writes=[scr], negate=True, tag="rt_max")
        p.tt("dve", ex.ap.rearrange("p (g e) -> p g e", e=32), PLv, nm_b, ALU.add,
             reads=[PL, scr], writes=[ex], tag="rt_sub")
        p.free(PL)
        p.act(exp_.ap, ex.ap, AF.Exp, reads=[ex], writes=[exp_], tag="rt_exp")
        expv = exp_.ap.rearrange("p (g e) -> p g e", e=32)
        cur = exp_
        for it in range(4):
            p.red(scr[:, 16 * (it + 2):16 * (it + 3)] if False else scr[:, 16:32], 
                  cur.ap.rearrange("p (g e) -> p g e", e=32), ALU.max,
                  reads=[cur], writes=[scr], tag="rt_it_max")
            if it < 3:
                mb = Prog.bcast_ap(scr, 16, NTC, 32)
                nxt = ex2 if cur is not ex2 else ex
                p.tt("dve", nxt.ap.rearrange("p (g e) -> p g e", e=32),
                     cur.ap.rearrange("p (g e) -> p g e", e=32), mb, ALU.is_lt,
                     reads=[cur, scr], writes=[nxt], tag="rt_it_ge")
                p.stt("dve", nxt.ap, nxt.ap, 1e30, cur.ap, ALU.mult, ALU.min,
                      reads=[cur, nxt], writes=[nxt], tag="rt_it_mask")
                cur = nxt
        th_b = Prog.bcast_ap(scr, 16, NTC, 32)
        p.tt("dve", ex2.ap.rearrange("p (g e) -> p g e", e=32) if cur is not ex2 else ex.ap.rearrange("p (g e) -> p g e", e=32),
             expv, th_b, ALU.is_ge, reads=[exp_, scr], writes=[ex2 if cur is not ex2 else ex], tag="rt_ge4")
        sel = ex2 if cur is not ex2 else ex
        p.tt("dve", sel.ap, sel.ap, exp_.ap, ALU.mult, reads=[sel, exp_], writes=[sel], tag="rt_gate0")
        p.red(scr[:, 32:48], expv, ALU.add, reads=[exp_], writes=[scr], tag="rt_sum")
        p.recip(scr[:, 48:64], scr[:, 32:48], reads=[scr], writes=[scr], tag="rt_rec")
        r_b = Prog.bcast_ap(scr, 48, NTC, 32)
        p.tt("dve", sel.ap.rearrange("p (g e) -> p g e", e=32), sel.ap.rearrange("p (g e) -> p g e", e=32),
             r_b, ALU.mult, reads=[sel, scr], writes=[sel], tag="rt_gate")
        PTp = PsumPool(p, "rt_pt", [32, 128], 2)
        for tc in range(NTC):
            PT = PTp.get()
            p.transp(PT.ap, sel[:, 32 * tc:32 * (tc + 1)], IDN.ap, reads=[sel, IDN], writes=[PT], tag="rt_tr")
            p.copy("dve", GFM[:, 128 * tc:128 * (tc + 1)], PT.ap, reads=[PT], writes=[GFM], tag="rt_ev")
        PTp.free()
        p.free(ex); p.free(ex2); p.free(exp_); p.free(scr)

        wslot = []
        for s in range(2):
            wslot.append(dict(
                w1=p.alloc(f"l0w1_{s}", [128, 8 * 512], BF16),
                w3=p.alloc(f"l0w3_{s}", [128, 8 * 512], BF16),
                w2=p.alloc(f"l0w2_{s}", [128, 4 * 1024], BF16),
            ))
        hg = [p.alloc(f"l0hg{i}", [128, T], BF16) for i in range(4)]
        Bsb = p.alloc("l0be", [128, T], F32)
        s1t = [p.alloc(f"l0s1_{i}", [128, SL], BF16) for i in range(2)]
        tbt = [p.alloc(f"l0tb_{i}", [128, SL], BF16) for i in range(2)]

        def load_l0w(e, s):
            w = wslot[s]
            if e < 32:
                srcs = (di["W1"].ap()[e], di["W3"].ap()[e], di["W2"].ap()[e])
            else:
                hb = e - 32
                srcs = (di["Ws1"].ap()[:, 512 * hb:512 * (hb + 1)],
                        di["Ws3"].ap()[:, 512 * hb:512 * (hb + 1)],
                        di["Ws2"].ap()[512 * hb:512 * (hb + 1), :])
            p.dma("pool", f"l0a{s}", w["w1"], srcs[0].rearrange("(c p) h -> p c h", p=128),
                  dst_ap=w["w1"].ap.rearrange("p (c h) -> p c h", c=8), bulk=False)
            p.dma("pool", f"l0b{s}", w["w3"], srcs[1].rearrange("(c p) h -> p c h", p=128),
                  dst_ap=w["w3"].ap.rearrange("p (c h) -> p c h", c=8), bulk=False)
            p.dma("pool", f"l0c{s}", w["w2"], srcs[2].rearrange("(c p) h -> p c h", p=128),
                  dst_ap=w["w2"].ap.rearrange("p (c h) -> p c h", c=4), bulk=False)

        P1p = PsumPool(p, "l0ps", [128, SL], 6)
        P3p = P1p
        PYp = P1p
        PBp = PsumPool(p, "l0pb", [128, SL], 1)

        load_l0w(0, 0)
        for e in range(34):
            s = e % 2
            w = wslot[s]
            if e + 1 < 34:
                load_l0w(e + 1, (e + 1) % 2)
            gated = e < 32
            if gated:
                for sl in range(NSL):
                    PB = PBp.get()
                    p.mm(PB.ap, SEL32[:, 128 * e:128 * (e + 1)], GFM[:, SL * sl:SL * (sl + 1)],
                         start=True, stop=True, reads=[SEL32, GFM], writes=[PB], tag="l0_bc")
                    p.copy("dve", Bsb[:, SL * sl:SL * (sl + 1)], PB.ap, reads=[PB], writes=[Bsb], tag="l0_bcev")
            w1v = w["w1"].ap.rearrange("p (c h) -> p c h", c=8)
            w3v = w["w3"].ap.rearrange("p (c h) -> p c h", c=8)
            w2v = w["w2"].ap.rearrange("p (c h) -> p c h", c=4)
            for hc in range(4):
                for sl in range(NSL):
                    P1 = P1p.get()
                    for c in range(DC):
                        p.mm(P1.ap, w1v[:, c, 128 * hc:128 * (hc + 1)], xln[c][:, SL * sl:SL * (sl + 1)],
                             start=(c == 0), stop=(c == DC - 1), reads=[w["w1"], xln[c]], writes=[P1], tag="l0_w1")
                    st = s1t[sl % 2]
                    p.act(st.ap, P1.ap, AF.Silu, reads=[P1], writes=[st], tag="l0_silu")
                    P3 = P3p.get()
                    for c in range(DC):
                        p.mm(P3.ap, w3v[:, c, 128 * hc:128 * (hc + 1)], xln[c][:, SL * sl:SL * (sl + 1)],
                             start=(c == 0), stop=(c == DC - 1), reads=[w["w3"], xln[c]], writes=[P3], tag="l0_w3")
                    tb = tbt[sl % 2]
                    if gated:
                        p.tt("dve", tb.ap, P3.ap, Bsb[:, SL * sl:SL * (sl + 1)], ALU.mult,
                             reads=[P3, Bsb], writes=[tb], tag="l0_tb")
                    else:
                        p.copy("dve", tb.ap, P3.ap, reads=[P3], writes=[tb], tag="l0_tb")
                    p.tt("pool", hg[hc][:, SL * sl:SL * (sl + 1)], st.ap, tb.ap, ALU.mult,
                         reads=[st, tb], writes=[hg[hc]], tag="l0_hg")
            for dc in range(DC):
                for sl in range(NSL):
                    PY = PYp.get()
                    for hc in range(4):
                        p.mm(PY.ap, w2v[:, hc, 128 * dc:128 * (dc + 1)], hg[hc][:, SL * sl:SL * (sl + 1)],
                             start=(hc == 0), stop=(hc == 3), reads=[w["w2"], hg[hc]], writes=[PY], tag="l0_w2")
                    p.stt("dve", x[dc][:, SL * sl:SL * (sl + 1)], PY.ap, 0.5, x[dc][:, SL * sl:SL * (sl + 1)],
                          ALU.mult, ALU.add, reads=[PY, x[dc]], writes=[x[dc]], tag="l0_acc")
        P1p.free(); PBp.free()
        p.free(xln); p.free(SEL32); p.free(GFM); p.free(hg); p.free(Bsb); p.free(s1t); p.free(tbt)
        for s in range(2):
            p.free(wslot[s])

    def emit_attn(layer):
        p.phase += 1
        ph = p.phase
        xln = emit_ln(x, BF16)
        for c in range(DC):
            p.dma("sp", f"sp{ph}", xsp[c], x[c])
        for c in range(DC):
            p.free(x[c])

        DMK = p.alloc("dmask", [128, 2048], F32)
        p.dma("sp", f"cs{ph}", DMK, di["c_dmask"].ap())
        if layer == 0:
            COS = p.alloc("cos", [128, T], BF16)
            SIN = p.alloc("sin", [128, T], BF16)
            SWP = p.alloc("swap", [128, 128], BF16)
            p.dma("pool", f"cb{ph}", COS, di["c_cos"].ap())
            p.dma("pool", f"cb{ph}", SIN, di["c_sin"].ap())
            p.dma("pool", f"cb{ph}", SWP, di["c_swap"].ap())
        else:
            BA = p.alloc("ba", [1, 3 * D], BF16)
            p.dma("pool", f"cb{ph}", BA, di["ba"].ap())

        q_b = [p.alloc(f"qb{c}", [128, T], BF16) for c in range(DC)]
        k_b = [p.alloc(f"kb{c}", [128, T], BF16) for c in range(DC)]
        vbuf = [p.alloc(f"vb{tc}", [128, 16 * 65], BF16) for tc in range(NTC)]
        for tc in range(NTC):
            p.memset("pool", vbuf[tc].ap.rearrange("p (h n) -> p h n", n=65)[:, :, 64:65], 1.0,
                     writes=[vbuf[tc]], tag="v_ones")

        wsl = [p.alloc(f"aw{s}", [128, 8 * 1024], BF16) for s in range(2)]

        def load_w(name, s, colslice=None):
            if layer == 0:
                src = di[name].ap()
            else:
                src = di["waT"].ap()[:, colslice] if colslice is not None else di[name].ap()
            p.dma("pool", f"aw{ph}{s}", wsl[s], src.rearrange("(c p) n -> p c n", p=128),
                  dst_ap=wsl[s].ap.rearrange("p (c n) -> p c n", c=8), bulk=False)

        if layer == 0:
            load_w("wq", 0)
            load_w("wk", 1)
        else:
            load_w(None, 0, colslice=slice(0, D))
            load_w(None, 1, colslice=slice(D, 2 * D))

        PQp = PsumPool(p, "a_pq", [128, SL], 4)

        def proj_fm(dsts, s, bias_off):
            wv_ = wsl[s].ap.rearrange("p (c n) -> p c n", c=8)
            for jc in range(DC):
                for sl in range(NSL):
                    PQ = PQp.get()
                    for c in range(DC):
                        p.mm(PQ.ap, wv_[:, c, 128 * jc:128 * (jc + 1)], xln[c][:, SL * sl:SL * (sl + 1)],
                             start=(c == 0), stop=(layer == 0 and c == DC - 1),
                             reads=[wsl[s], xln[c]], writes=[PQ], tag="a_proj")
                    if layer == 1:
                        p.mm(PQ.ap, BA[0:1, bias_off + 128 * jc: bias_off + 128 * (jc + 1)],
                             ONESROW512B, start=False, stop=True, reads=[BA, ONESRB], writes=[PQ], tag="a_bias")
                    p.copy("dve", dsts[jc][:, SL * sl:SL * (sl + 1)], PQ.ap, reads=[PQ], writes=[dsts[jc]], tag="a_qev")

        if layer == 0:
            proj_fm(q_b, 0, 0)
            load_w("wv", 0)
            proj_fm(k_b, 1, 0)
        else:
            proj_fm(q_b, 0, 0)
            load_w(None, 0, colslice=slice(2 * D, 3 * D))
            proj_fm(k_b, 1, D)

        wv_ = wsl[0].ap.rearrange("p (c n) -> p c n", c=8)
        for tc in range(NTC):
            for half in range(2):
                PV_ = PQp.get()
                for c in range(DC):
                    p.mm(PV_.ap, xln[c][:, 128 * tc:128 * (tc + 1)], wv_[:, c, SL * half:SL * (half + 1)],
                         start=(c == 0), stop=(layer == 0 and c == DC - 1),
                         reads=[wsl[0], xln[c]], writes=[PV_], tag="a_v")
                if layer == 1:
                    p.mm(PV_.ap, ONESROW128B, BA[0:1, 2 * D + SL * half: 2 * D + SL * (half + 1)],
                         start=False, stop=True, reads=[ONESRB, BA], writes=[PV_], tag="a_vb")
                p.copy("dve", vbuf[tc].ap.rearrange("p (h n) -> p h n", n=65)[:, 8 * half:8 * (half + 1), 0:64],
                       PV_.ap.rearrange("p (h n) -> p h n", n=64), reads=[PV_], writes=[vbuf[tc]], tag="a_vev")

        if layer == 0:
            ur = p.alloc("rope_u", [128, SL], F32)
            t1r = p.alloc("rope_t1", [128, SL], F32)
            PSp = PsumPool(p, "a_sw", [128, SL], 4)
            for tiles in (q_b, k_b):
                for jc in range(DC):
                    for sl in range(NSL):
                        PS = PSp.get()
                        p.mm(PS.ap, SWP.ap, tiles[jc][:, SL * sl:SL * (sl + 1)], start=True, stop=True,
                             reads=[SWP, tiles[jc]], writes=[PS], tag="rope_mm")
                        p.tt("dve", ur.ap, PS.ap, SIN[:, SL * sl:SL * (sl + 1)], ALU.mult,
                             reads=[PS, SIN], writes=[ur], tag="rope_u")
                        p.tt("pool", t1r.ap, tiles[jc][:, SL * sl:SL * (sl + 1)], COS[:, SL * sl:SL * (sl + 1)],
                             ALU.mult, reads=[tiles[jc], COS], writes=[t1r], tag="rope_t1")
                        p.tt("dve", tiles[jc][:, SL * sl:SL * (sl + 1)], t1r.ap, ur.ap, ALU.add,
                             reads=[t1r, ur], writes=[tiles[jc]], tag="rope_o")
            PSp.free()
            p.free(ur); p.free(t1r)
            p.free(COS); p.free(SIN); p.free(SWP)
        PQp.free()

        p.free(xln)
        p.free(wsl)
        if layer == 1:
            p.free(BA)
        o_all = [p.alloc(f"oa{c}", [128, T], BF16) for c in range(DC)]
        pT = [p.alloc(f"pt{i}", [128, SL], BF16) for i in range(4)]
        r_row = p.alloc("a_rrow", [1, T], F32)
        o_raw = p.alloc("a_oraw", [64, T], BF16)
        rb_sb = p.alloc("a_rbc", [64, T], BF16)

        pti = 0
        for h in range(16):
            kc = h // 2
            base = 64 * (h % 2)
            PVp = p.palloc("a_pvps", [65, T])
            STpool = PsumPool(p, "a_st", [128, SL], 3)
            slices = []
            for c in range(NTC):
                for qs in range(c // 4, NSL):
                    slices.append((c, qs))
            pend = []
            for i in range(len(slices) + 2):
                if i < len(slices):
                    c, qs = slices[i]
                    ST = STpool.get()
                    p.mm(ST.ap,
                         k_b[kc][base:base + 64, 128 * c:128 * (c + 1)],
                         q_b[kc][base:base + 64, SL * qs:SL * (qs + 1)],
                         start=True, stop=True, reads=[k_b[kc], q_b[kc]], writes=[ST], tag="a_score")
                    if qs == c // 4:
                        p.tt("dve", ST.ap, ST.ap, DMK[:, SL * (c % 4):SL * (c % 4 + 1)],
                             ALU.add, reads=[DMK, ST], writes=[ST], tag="a_mask")
                    pt = pT[pti % 4]
                    pti += 1
                    p.act(pt.ap, ST.ap, AF.Exp, reads=[ST], writes=[pt], scale=0.125, tag="a_exp")
                    pend.append((c, qs, pt))
                if len(pend) > 2 or (i >= len(slices) and pend):
                    c2, qs2, pt2 = pend.pop(0)
                    p.mm(PVp[0:65, SL * qs2:SL * (qs2 + 1)],
                         vbuf[c2].ap.rearrange("p (hh n) -> p hh n", n=65)[:, h, :],
                         pt2.ap,
                         start=(c2 == 0), stop=(c2 == 4 * qs2 + 3), reads=[vbuf[c2], pt2], writes=[PVp], tag="a_pv")
            p.recip(r_row.ap, PVp[64:65, :], reads=[PVp], writes=[r_row], tag="a_recip")
            p.act(o_raw.ap, PVp[0:64, :], AF.Copy, reads=[PVp], writes=[o_raw], tag="a_oraw")
            STpool.free()
            RBC = p.palloc("a_rbcp", [64, T])
            for sl in range(NSL):
                p.mm(RBC[:, SL * sl:SL * (sl + 1)], ONESR[0:1, 0:64], r_row[0:1, SL * sl:SL * (sl + 1)],
                     start=True, stop=True, reads=[ONESR, r_row], writes=[RBC], tag="a_rbcmm")
            p.copy("dve", rb_sb.ap, RBC.ap, reads=[RBC], writes=[rb_sb], tag="a_rbcev")
            p.free(RBC)
            p.tt("pool", o_all[kc][base:base + 64, :], o_raw.ap, rb_sb.ap, ALU.mult,
                 reads=[o_raw, rb_sb], writes=[o_all[kc]], tag="a_onorm")
            p.free(PVp)

        p.free(q_b); p.free(k_b); p.free(vbuf); p.free(pT)
        p.free(r_row); p.free(o_raw); p.free(rb_sb); p.free(DMK)

        wo = p.alloc("awo", [128, 8 * 1024], BF16)
        p.dma("pool", f"awo{ph}", wo, di["wo" if layer == 0 else "wpT"].ap().rearrange("(c p) n -> p c n", p=128),
              dst_ap=wo.ap.rearrange("p (c n) -> p c n", c=8), bulk=False)
        BP = None
        if layer == 1:
            BP = p.alloc("bp", [1, D], BF16)
            p.dma("pool", f"cb2{ph}", BP, di["bp"].ap())
        xn = [p.alloc(f"xn{c}", [128, T], F32) for c in range(DC)]
        for c in range(DC):
            p.dma("sp", f"xr{ph}", xn[c], xsp[c])
        wov = wo.ap.rearrange("p (c n) -> p c n", c=8)
        POp = PsumPool(p, "a_po", [128, SL], 4)
        for dc in range(DC):
            for sl in range(NSL):
                PO = POp.get()
                for c in range(DC):
                    p.mm(PO.ap, wov[:, c, 128 * dc:128 * (dc + 1)], o_all[c][:, SL * sl:SL * (sl + 1)],
                         start=(c == 0), stop=(layer == 0 and c == DC - 1),
                         reads=[wo, o_all[c]], writes=[PO], tag="a_wo")
                if layer == 1:
                    p.mm(PO.ap, BP[0:1, 128 * dc:128 * (dc + 1)], ONESROW512B,
                         start=False, stop=True, reads=[BP, ONESRB], writes=[PO], tag="a_wob")
                p.tt("dve", xn[dc][:, SL * sl:SL * (sl + 1)], PO.ap, xn[dc][:, SL * sl:SL * (sl + 1)],
                     ALU.add, reads=[PO, xn[dc]], writes=[xn[dc]], tag="a_res")
        POp.free()
        p.free(o_all); p.free(wo)
        if BP is not None:
            p.free(BP)
        for c in range(DC):
            x[c] = xn[c]

    def emit_l1_ffn():
        p.phase += 1
        ph = p.phase
        t = emit_ln(x, F32)
        for c in range(DC):
            p.dma("sp", f"sp{ph}", xsp[c], x[c])
        for c in range(DC):
            p.free(x[c])

        CFM = p.alloc("cfm", [4, T], BF16)
        for l in range(4):
            tb = [p.alloc(f"l1tb{c}", [128, T], BF16) for c in range(DC)]
            for c in range(DC):
                p.copy("pool", tb[c].ap, t[c].ap, reads=[t[c]], writes=[tb[c]], tag="l1_tb")
            ex = p.alloc("l1_ex", [128, 64], F32)
            ex2 = p.alloc("l1_ex2", [128, 64], F32)
            exp_ = p.alloc("l1_exp", [128, 64], F32)
            scr = p.alloc("l1_scr", [128, 96], F32)
            PL = p.palloc("l1_psl", [128, 64])
            n = 0
            for tc in range(NTC):
                for c in range(DC):
                    n += 1
                    p.mm(PL[:, 4 * tc:4 * (tc + 1)], tb[c][:, 128 * tc:128 * (tc + 1)],
                         RWB[:, 32 * l + 4 * c:32 * l + 4 * (c + 1)],
                         start=(n == 1), stop=False, reads=[tb[c], RWB], writes=[PL], tag="l1_rt")
            p.mm(PL.ap, ONESROW128, RB16[0:1, 64 * l:64 * (l + 1)], start=False, stop=True,
                 reads=[ONESR, RB16], writes=[PL], tag="l1_rtb")
            PLv = PL.ap.rearrange("p (g e) -> p g e", e=4)
            p.red(scr[:, 0:16], PLv, ALU.max, reads=[PL], writes=[scr], negate=True, tag="l1_max")
            nm_b = Prog.bcast_ap(scr, 0, NTC, 4)
            p.tt("dve", ex.ap.rearrange("p (g e) -> p g e", e=4), PLv, nm_b, ALU.add,
                 reads=[PL, scr], writes=[ex], tag="l1_sub")
            p.free(PL)
            p.act(exp_.ap, ex.ap, AF.Exp, reads=[ex], writes=[exp_], tag="l1_exp")
            expv = exp_.ap.rearrange("p (g e) -> p g e", e=4)
            p.red(scr[:, 16:32], expv, ALU.max, reads=[exp_], writes=[scr], tag="l1_m1")
            m_b = Prog.bcast_ap(scr, 16, NTC, 4)
            p.tt("dve", ex2.ap.rearrange("p (g e) -> p g e", e=4), expv, m_b, ALU.is_lt,
                 reads=[exp_, scr], writes=[ex2], tag="l1_lt")
            p.stt("dve", ex2.ap, ex2.ap, 1e30, exp_.ap, ALU.mult, ALU.min,
                  reads=[exp_, ex2], writes=[ex2], tag="l1_mask1")
            p.red(scr[:, 16:32], ex2.ap.rearrange("p (g e) -> p g e", e=4), ALU.max,
                  reads=[ex2], writes=[scr], tag="l1_m2")
            th_b = Prog.bcast_ap(scr, 16, NTC, 4)
            p.tt("dve", ex2.ap.rearrange("p (g e) -> p g e", e=4), expv, th_b, ALU.is_ge,
                 reads=[exp_, scr], writes=[ex2], tag="l1_ge2")
            p.tt("dve", ex2.ap, ex2.ap, exp_.ap, ALU.mult, reads=[ex2, exp_], writes=[ex2], tag="l1_vals")
            p.red(scr[:, 32:48], ex2.ap.rearrange("p (g e) -> p g e", e=4), ALU.add,
                  reads=[ex2], writes=[scr], tag="l1_vsum")
            p.recip(scr[:, 48:64], scr[:, 32:48], reads=[scr], writes=[scr], tag="l1_vrec")
            r_b = Prog.bcast_ap(scr, 48, NTC, 4)
            p.tt("dve", ex2.ap.rearrange("p (g e) -> p g e", e=4), ex2.ap.rearrange("p (g e) -> p g e", e=4),
                 r_b, ALU.mult, reads=[ex2, scr], writes=[ex2], tag="l1_cnorm")
            PTp = PsumPool(p, "l1_ct", [4, 128], 2)
            for tc in range(NTC):
                PT = PTp.get()
                p.transp(PT.ap, ex2[:, 4 * tc:4 * (tc + 1)], IDN.ap, reads=[ex2, IDN], writes=[PT], tag="l1_ctr")
                p.copy("dve", CFM[:, 128 * tc:128 * (tc + 1)], PT.ap, reads=[PT], writes=[CFM], tag="l1_cev")
            PTp.free()
            p.free(tb); p.free(ex); p.free(ex2); p.free(exp_); p.free(scr)

            xhat = emit_ln(t, BF16)

            Ssb = p.alloc("l1_S", [128, T], F32)
            PSp = PsumPool(p, "l1_ps", [128, SL], 3)
            for sl in range(NSL):
                PS_ = PSp.get()
                p.mm(PS_.ap, ONES4.ap, CFM[:, SL * sl:SL * (sl + 1)], start=True, stop=True,
                     reads=[ONES4, CFM], writes=[PS_], tag="l1_smm")
                p.copy("dve", Ssb[:, SL * sl:SL * (sl + 1)], PS_.ap, reads=[PS_], writes=[Ssb], tag="l1_sev")
            for dc in range(DC):
                for sl in range(NSL):
                    p.tt("pool", t[dc][:, SL * sl:SL * (sl + 1)], t[dc][:, SL * sl:SL * (sl + 1)],
                         Ssb[:, SL * sl:SL * (sl + 1)], ALU.mult, reads=[Ssb, t[dc]], writes=[t[dc]], tag="l1_sc")
            p.free(Ssb)
            for dc in range(DC):
                for sl in range(NSL):
                    PS_ = PSp.get()
                    p.mm(PS_.ap, B2L[:, 1024 * l + 128 * dc:1024 * l + 128 * (dc + 1)],
                         CFM[:, SL * sl:SL * (sl + 1)], start=True, stop=True,
                         reads=[B2L, CFM], writes=[PS_], tag="l1_b2")
                    p.tt("dve", t[dc][:, SL * sl:SL * (sl + 1)], PS_.ap, t[dc][:, SL * sl:SL * (sl + 1)],
                         ALU.add, reads=[PS_, t[dc]], writes=[t[dc]], tag="l1_b2a")
            PSp.free()

            w1s = [p.alloc(f"l1w1_{s}", [128, 8 * 1024], BF16) for s in range(2)]
            w2s = [p.alloc("l1w2_0", [128, 8 * 1024], BF16)]
            Bsb = p.alloc("l1_be", [128, T], F32)
            xet = [p.alloc(f"l1xe{i}", [128, 8 * SL], BF16) for i in range(2)]
            hts = [p.alloc(f"l1h{i}", [128, SL], BF16) for i in range(8)]
            PHp = PsumPool(p, "l1_ph", [128, SL], 3)
            PYp = PsumPool(p, "l1_py", [128, SL], 3)
            PBp = PsumPool(p, "l1_pb", [128, SL], 1)

            def load_l1w1(e, fb, s):
                p.dma("pool", f"l1a{s}", w1s[s],
                      di["w1"].ap()[l, e][:, 1024 * fb:1024 * (fb + 1)].rearrange("(c p) f -> p c f", p=128),
                      dst_ap=w1s[s].ap.rearrange("p (c f) -> p c f", c=8), bulk=False)

            def load_l1w2(e, fb):
                p.dma("pool", "l1b0", w2s[0],
                      di["w2"].ap()[l, e][1024 * fb:1024 * (fb + 1), :].rearrange("(c p) h -> p c h", p=128),
                      dst_ap=w2s[0].ap.rearrange("p (c h) -> p c h", c=8), bulk=False)

            load_l1w1(0, 0, 0)
            load_l1w2(0, 0)
            step = 0
            for e in range(4):
                for sl in range(NSL):
                    PB = PBp.get()
                    p.mm(PB.ap, SEL4[:, 128 * e:128 * (e + 1)], CFM[:, SL * sl:SL * (sl + 1)],
                         start=True, stop=True, reads=[SEL4, CFM], writes=[PB], tag="l1_bc")
                    p.copy("dve", Bsb[:, SL * sl:SL * (sl + 1)], PB.ap, reads=[PB], writes=[Bsb], tag="l1_bcev")
                def emit_xe(xbuf, ee, tc):
                    xev = xbuf.ap.rearrange("p (c n) -> p c n", c=8)
                    for c in range(DC):
                        p.ts("dve", xev[:, c, :], xhat[c][:, SL * tc:SL * (tc + 1)],
                             GCOLS[:, 32 * l + 8 * ee + c:32 * l + 8 * ee + c + 1],
                             BCOLS[:, 32 * l + 8 * ee + c:32 * l + 8 * ee + c + 1],
                             ALU.mult, ALU.add, reads=[xhat[c], GCOLS, BCOLS], writes=[xbuf], tag="l1_xe")

                for fb in range(4):
                    s = step % 2
                    if step + 1 < 16:
                        e2, fb2 = (e, fb + 1) if fb + 1 < 4 else (e + 1, 0)
                        load_l1w1(e2, fb2, (step + 1) % 2)
                    w1v = w1s[s].ap.rearrange("p (c f) -> p c f", c=8)
                    w2v = w2s[0].ap.rearrange("p (c h) -> p c h", c=8)
                    if e == 0 and fb == 0:
                        emit_xe(xet[0], e, 0)
                    for tc in range(NSL):
                        xe = xet[tc % 2]
                        xev = xe.ap.rearrange("p (c n) -> p c n", c=8)
                        for fc in range(8):
                            PH = PHp.get()
                            for c in range(DC):
                                p.mm(PH.ap, w1v[:, c, 128 * fc:128 * (fc + 1)], xev[:, c, :],
                                     start=(c == 0), stop=(c == DC - 1), reads=[w1s[s], xe], writes=[PH], tag="l1_w1")
                            p.act(hts[fc].ap, PH.ap, AF.Gelu,
                                  reads=[PH, B1COLS], writes=[hts[fc]],
                                  bias=B1COLS[:, 128 * l + 32 * e + 8 * fb + fc:128 * l + 32 * e + 8 * fb + fc + 1],
                                  tag="l1_gelu")
                            p.tt("pool", hts[fc].ap, hts[fc].ap, Bsb[:, SL * tc:SL * (tc + 1)], ALU.mult,
                                 reads=[Bsb, hts[fc]], writes=[hts[fc]], tag="l1_hg")
                        if tc + 1 < NSL:
                            emit_xe(xet[(tc + 1) % 2], e, tc + 1)
                        elif step + 1 < 16:
                            emit_xe(xet[0], e2, 0)
                        for dc in range(DC):
                            PY = PYp.get()
                            for fc in range(8):
                                p.mm(PY.ap, w2v[:, fc, 128 * dc:128 * (dc + 1)], hts[fc].ap,
                                     start=(fc == 0), stop=(fc == 7), reads=[w2s[0], hts[fc]], writes=[PY], tag="l1_w2")
                            p.tt("dve", t[dc][:, SL * tc:SL * (tc + 1)], PY.ap, t[dc][:, SL * tc:SL * (tc + 1)],
                                 ALU.add, reads=[PY, t[dc]], writes=[t[dc]], tag="l1_acc")
                    if step + 1 < 16:
                        load_l1w2(e2, fb2)
                    step += 1
            PHp.free(); PYp.free(); PBp.free()
            p.free(w1s); p.free(w2s); p.free(Bsb); p.free(xet); p.free(hts); p.free(xhat)
        p.free(CFM)
        xn = [p.alloc(f"xf{c}", [128, T], F32) for c in range(DC)]
        for c in range(DC):
            p.dma("sp", f"xr{ph}", xn[c], xsp[c])
        for c in range(DC):
            p.stt("dve", xn[c].ap, t[c].ap, 0.5, xn[c].ap, ALU.mult, ALU.add,
                  reads=[t[c], xn[c]], writes=[xn[c]], tag="l1_fin")
        p.free(t)
        for c in range(DC):
            x[c] = xn[c]

    for part in parts:
        if part == "l0ffn":
            emit_l0_ffn()
        elif part == "l0attn":
            emit_attn(0)
        elif part == "l1attn":
            emit_attn(1)
        elif part == "l1ffn":
            emit_l1_ffn()
        else:
            raise ValueError(part)

    for c in range(DC):
        ot = p.dram_tile(f"out{c}", OUT.ap()[128 * c:128 * (c + 1), :])
        p.dma("sp", "ost", ot, x[c])
    n_out = p.dma_count["ost"] * 16
    p.op("sp", lambda e, _s: e.wait_ge(_s["ost"], n_out), reads=(), writes=(), tag="final")

    p.emit(es, nc)
    return nc, es, p



def _consts():
    c = {}
    cm = np.zeros((128, 8), np.float32)
    cm[:, 0] = 1.0
    cm[:, 1] = 1e-5
    c["c_misc"] = cm
    c["c_onesrow"] = np.ones((1, 640), np.float32)
    c["c_onesrow_bf"] = np.ones((1, 640), np.float32)
    c["c_idn"] = np.eye(128, dtype=np.float32)
    sel4 = np.zeros((4, 512), np.float32)
    for e in range(4):
        sel4[e, 128 * e:128 * (e + 1)] = 1.0
    c["c_sel4"] = sel4
    sel32 = np.zeros((32, 4096), np.float32)
    for e in range(32):
        sel32[e, 128 * e:128 * (e + 1)] = 1.0
    c["c_sel32"] = sel32
    c["c_ones4"] = np.ones((4, 128), np.float32)
    swap = np.zeros((128, 128), np.float32)
    for i in range(0, 128, 2):
        swap[i + 1, i] = 1.0
        swap[i, i + 1] = 1.0
    c["c_swap"] = swap
    hd = 64
    inv = 1.0 / (10000.0 ** (np.arange(0, hd, 2, dtype=np.float64) / hd))
    pos = np.arange(T, dtype=np.float64)
    cos = np.zeros((128, T), np.float64)
    sin = np.zeros((128, T), np.float64)
    for pp in range(128):
        f = inv[(pp % 64) // 2]
        cos[pp] = np.cos(pos * f)
        sgn = -1.0 if pp % 2 == 0 else 1.0
        sin[pp] = sgn * np.sin(pos * f)
    c["c_cos"] = cos.astype(np.float32)
    c["c_sin"] = sin.astype(np.float32)
    dm = np.zeros((128, 2048), np.float32)
    for m in range(4):
        k = np.arange(128)[:, None]
        j = np.arange(512)[None, :]
        dm[:, 512 * m:512 * (m + 1)] = np.where(128 * m + k <= j, 0.0, NEG_BIG)
    c["c_dmask"] = dm
    return c


def host_inputs(A):
    shared = _consts()
    shared.update(dict(
        wq=A["L0_attn_wq"], wk=A["L0_attn_wk"], wv=A["L0_attn_wv"], wo=A["L0_attn_wo"],
        Wg=A["L0_ffn_Wg"], W1=A["L0_ffn_W1"], W3=A["L0_ffn_W3"], W2=A["L0_ffn_W2"],
        Ws1=A["L0_ffn_Ws1"], Ws3=A["L0_ffn_Ws3"], Ws2=A["L0_ffn_Ws2"],
        waT=np.ascontiguousarray(np.asarray(A["L1_attn_w_attn"]).T),
        ba=np.asarray(A["L1_attn_b_attn"]).reshape(1, -1),
        wpT=np.ascontiguousarray(np.asarray(A["L1_attn_w_proj"]).T),
        bp=np.asarray(A["L1_attn_b_proj"]).reshape(1, -1),
        rw=A["L1_ffn_router_w"],
        rb=np.asarray(A["L1_ffn_router_b"]).reshape(1, 16),
        rb16=np.ascontiguousarray(np.tile(np.asarray(A["L1_ffn_router_b"]).reshape(4, 1, 4), (1, 16, 1)).reshape(1, 256)),
        gcols=np.ascontiguousarray(np.asarray(A["L1_ffn_ln_g"]).reshape(4, 4, 8, 128).transpose(3, 0, 1, 2).reshape(128, 128)),
        bcols=np.ascontiguousarray(np.asarray(A["L1_ffn_ln_b"]).reshape(4, 4, 8, 128).transpose(3, 0, 1, 2).reshape(128, 128)),
        b1cols=np.ascontiguousarray(np.asarray(A["L1_ffn_b1"]).reshape(4, 4, 32, 128).transpose(3, 0, 1, 2).reshape(128, 512)),
        w1=A["L1_ffn_w1"], w2=A["L1_ffn_w2"],
        b2l=np.ascontiguousarray(np.asarray(A["L1_ffn_b2"]).transpose(1, 0, 2).reshape(4, 4096)),
    ))
    return {k: np.ascontiguousarray(np.asarray(v, np.float32)) for k, v in shared.items()}


FULL_PARTS = ["l0ffn", "l0attn", "l0ffn", "l1ffn", "l1attn", "l1ffn"]


def run_parts(xfull, shared, parts):
    nc, es, p = build_nc(parts)
    in_maps = []
    for i in range(8):
        m = dict(shared)
        m["xin"] = np.ascontiguousarray(np.asarray(xfull[i], np.float32).T)
        in_maps.append(m)
    res = run_bass_kernel_spmd(nc, in_maps, core_ids=list(range(8)))
    es.close()
    return np.ascontiguousarray(np.stack([np.asarray(r["out"], np.float32).T for r in res.results], 0))


def kernel(**inputs):
    A = {k: np.asarray(v) for k, v in inputs.items()}
    shared = host_inputs(A)
    return run_parts(np.asarray(A["L0_attn_x"], np.float32), shared, FULL_PARTS)


# version 08
# speedup vs baseline: 1.0361x; 1.0361x over previous
import numpy as np
import ml_dtypes

import concourse.bass as bass
import concourse.mybir as mybir
from concourse.bass_utils import run_bass_kernel_spmd

F32 = mybir.dt.float32
BF16 = mybir.dt.bfloat16
U8 = mybir.dt.uint8
AF = mybir.ActivationFunctionType
ALU = mybir.AluOpType
AX = mybir.AxisListType

T = 2048
D = 1024
DC = 8
NSL = 4
SL = 512
NTC = 16
NEG_BIG = -1.0e9

ENGINES = ("pe", "act", "dve", "pool", "sp")
SAME_ENGINE_SYNC = {"act", "dve", "pool", "sp"}

PSUM_F32 = 4096


class Tile:
    __slots__ = ("name", "ap_full", "rng", "space", "lw", "readers", "prog")

    def __init__(self, prog, name, ap_full, rng, space):
        self.prog = prog
        self.name = name
        self.ap_full = ap_full
        self.rng = rng
        self.space = space
        self.lw = None
        self.readers = []

    def __getitem__(self, idx):
        return self.ap_full[idx]

    @property
    def ap(self):
        return self.ap_full


class Op:
    __slots__ = ("engine", "idx", "fn", "clock", "inc", "waits", "tag")

    def __init__(self, engine, idx, fn, clock, tag=""):
        self.engine = engine
        self.idx = idx
        self.fn = fn
        self.clock = clock
        self.inc = False
        self.waits = []
        self.tag = tag


class DmaEv:
    __slots__ = ("key", "count", "clock", "bulk")

    def __init__(self, key, count, clock, bulk):
        self.key = key
        self.count = count
        self.clock = clock
        self.bulk = bulk


class Prog:
    def __init__(self):
        self.ops = {e: [] for e in ENGINES}
        self.clock = {e: {} for e in ENGINES}
        self.dma_count = {}
        self.sem_keys = []
        self.sb_free = None
        self.ps_free = [(0, PSUM_F32 * 4)]
        self.sb_hist = []
        self.ps_hist = []
        self.live = {}
        self.nwaits = 0
        self.phase = 0
        self.nc = None
        self.sb = None
        self.ps = None

    def _alloc(self, free, nbytes, align):
        for i, (a, b) in enumerate(free):
            a2 = (a + align - 1) // align * align
            if b - a2 >= nbytes:
                if a2 > a:
                    free[i] = (a, a2)
                    free.insert(i + 1, (a2 + nbytes, b))
                else:
                    free[i] = (a + nbytes, b)
                if free[i][0] >= free[i][1]:
                    free.pop(i)
                return a2
        raise MemoryError(f"arena OOM for {getattr(self, chr(95)+chr(108)+chr(97)+chr(115)+chr(116)+chr(95)+chr(97)+chr(108)+chr(108)+chr(111)+chr(99), chr(63))}: need {nbytes}, free={free}, live={ {k: v for k, v in sorted(getattr(self, chr(108)+chr(105)+chr(118)+chr(101), {}).items()) if v} }")

    def _collect_hist(self, hist, rng):
        evs = []
        keep = []
        for r, e in hist:
            if r[0] < rng[1] and rng[0] < r[1]:
                evs.extend(e)
            else:
                keep.append((r, e))
        hist[:] = keep
        return evs

    def alloc(self, name, shape, dtype, align=64):
        esz = 2 if dtype == BF16 else 4
        nbytes = shape[-1] * esz
        self._last_alloc = name
        off = self._alloc(self.sb_free, nbytes, align)
        view = self.sb.bitcast(dtype)
        e0 = off // esz
        ap = view[0:shape[0], e0:e0 + shape[-1]]
        t = Tile(self, name, ap, (off, off + nbytes), "sb")
        self.live[name] = self.live.get(name, 0) + nbytes
        evs = self._collect_hist(self.sb_hist, t.rng)
        if evs:
            t.readers = evs
        return t

    def palloc(self, name, shape):
        nbytes = (shape[-1] * 4 + 2047) // 2048 * 2048
        off = self._alloc(self.ps_free, nbytes, 2048)
        e0 = off // 4
        ap = self.ps[0:shape[0], e0:e0 + shape[-1]]
        t = Tile(self, name, ap, (off, off + nbytes), "ps")
        evs = self._collect_hist(self.ps_hist, t.rng)
        if evs:
            t.readers = evs
        return t

    def free(self, t):
        if isinstance(t, (list, tuple)):
            for y in t:
                self.free(y)
            return
        if isinstance(t, dict):
            for y in t.values():
                self.free(y)
            return
        free, hist = (self.sb_free, self.sb_hist) if t.space == "sb" else (self.ps_free, self.ps_hist)
        if t.space == "sb":
            self.live[t.name] = self.live.get(t.name, 0) - (t.rng[1] - t.rng[0])
        evs = list(t.readers)
        if t.lw is not None:
            evs.append(t.lw)
        if evs:
            hist.append((t.rng, evs))
        a, b = t.rng
        free.append((a, b))
        free.sort()
        merged = []
        for r in free:
            if merged and r[0] <= merged[-1][1]:
                merged[-1] = (merged[-1][0], max(merged[-1][1], r[1]))
            else:
                merged.append(r)
        free[:] = merged

    def dram_tile(self, name, ap):
        return Tile(self, name, ap, None, "dram")

    @staticmethod
    def bcast_ap(tile, col_off, groups, reps):
        base = tile.ap_full
        pat = [list(base.ap[0]), [1, groups], [0, reps]]
        return bass.AP(base.tensor, base.offset + col_off, pat)

    def _need(self, myeng, myclock, ev):
        if isinstance(ev, DmaEv):
            if myclock.get("sem:" + ev.key, 0) >= ev.count:
                return None
            return ev
        if ev.engine == myeng == "pe":
            return None
        if myclock.get(ev.engine, -1) >= ev.idx:
            return None
        return ev

    def op(self, engine, fn, reads=(), writes=(), tag="", register=True):
        myclock = self.clock[engine]
        idx = len(self.ops[engine])
        needed = []
        seen = set()

        def add(ev):
            r = self._need(engine, myclock, ev)
            if r is not None and id(r) not in seen:
                seen.add(id(r))
                needed.append(r)

        for t in reads:
            if t.lw is not None:
                add(t.lw)
        for t in writes:
            if t.lw is not None:
                add(t.lw)
            for r in t.readers:
                add(r)
        if len(needed) > 1:
            best = {}
            for ev in needed:
                k = ("s", ev.key) if isinstance(ev, DmaEv) else ("e", ev.engine)
                cur = best.get(k)
                v = ev.count if isinstance(ev, DmaEv) else ev.idx
                if cur is None or v > cur[0]:
                    best[k] = (v, ev)
            needed = [v[1] for v in best.values()]
        newclock = dict(myclock)
        for ev in needed:
            for k, v in ev.clock.items():
                if newclock.get(k, -1) < v:
                    newclock[k] = v
            if isinstance(ev, DmaEv):
                k = "sem:" + ev.key
                newclock[k] = max(newclock.get(k, 0), ev.count)
            else:
                ev.inc = True
                newclock[ev.engine] = max(newclock.get(ev.engine, -1), ev.idx)
        o = Op(engine, idx, fn, newclock, tag)
        o.waits = needed
        self.nwaits += len(needed)
        self.ops[engine].append(o)
        self.clock[engine] = newclock
        if register:
            for t in reads:
                t.readers.append(o)
            for t in writes:
                t.lw = o
                t.readers = []
        return o

    def dma(self, queue, key, dst, src, dst_ap=None, src_ap=None, bulk=True, tag=""):
        if key not in self.dma_count:
            self.dma_count[key] = 0
            self.sem_keys.append(key)
        self.dma_count[key] += 1
        cnt = self.dma_count[key] * 16
        d_ap = dst_ap if dst_ap is not None else dst.ap
        s_ap = src_ap if src_ap is not None else (src.ap if isinstance(src, Tile) else src)

        def fn(eng, _sems, _k=key, _d=d_ap, _s=s_ap):
            return eng.dma_start(_d, _s).then_inc(_sems[_k], 16)

        fn._dma = True
        reads = [src] if isinstance(src, Tile) else []
        writes = [dst]
        o = self.op(queue, fn, reads=reads, writes=writes, tag=tag or f"dma:{key}",
                    register=False)
        ev = DmaEv(key, cnt, o.clock, bulk)
        dst.lw = ev
        dst.readers = []
        if isinstance(src, Tile):
            src.readers.append(ev)
        return ev

    def emit(self, es, nc):
        sems = {}
        for e in ENGINES:
            sems[e] = es.enter_context(nc.semaphore(f"sem_{e}"))
        for k in self.sem_keys:
            sems[k] = es.enter_context(nc.semaphore(f"d_{k}"))
        incno = {}
        for e in ENGINES:
            c = 0
            nos = []
            for o in self.ops[e]:
                if o.inc:
                    c += 1
                nos.append(c)
            incno[e] = nos

        final = {k: v * 16 for k, v in self.dma_count.items()}

        def run_engine(eng_obj, ename):
            last_wait = {}
            for o in self.ops[ename]:
                for ev in o.waits:
                    if isinstance(ev, DmaEv):
                        sname = ev.key
                        val = final[ev.key] if ev.bulk else ev.count
                    else:
                        sname = ev.engine
                        val = incno[ev.engine][ev.idx]
                    if last_wait.get(sname, -1) >= val:
                        continue
                    last_wait[sname] = val
                    eng_obj.wait_ge(sems[sname], val)
                inst = o.fn(eng_obj, sems)
                if o.inc and not getattr(o.fn, "_dma", False):
                    inst.then_inc(sems[ename], 1)

        def run_sp(s):
            run_engine(s, "sp")
            for k in self.sem_keys:
                s.wait_ge(sems[k], final[k])

        block = es.enter_context(nc.Block())
        block.tensor(lambda t: run_engine(t, "pe"))
        block.scalar(lambda s: run_engine(s, "act"))
        block.vector(lambda v: run_engine(v, "dve"))
        block.gpsimd(lambda g: run_engine(g, "pool"))
        block.sync(run_sp)

    def mm(self, out_ap, lhsT_ap, rhs_ap, start, stop, reads, writes, tag=""):
        return self.op("pe", lambda e, _s, _o=out_ap, _l=lhsT_ap, _r=rhs_ap, _st=start, _sp=stop:
                       e.matmul(_o, _l, _r, start=_st, stop=_sp), reads=reads, writes=writes, tag=tag)

    def transp(self, out_ap, in_ap, idn_ap, reads, writes, tag="tr"):
        return self.op("pe", lambda e, _s, _o=out_ap, _i=in_ap, _d=idn_ap: e.transpose(_o, _i, _d),
                       reads=reads, writes=writes, tag=tag)

    def act(self, out_ap, in_ap, func, reads, writes, bias=0.0, scale=1.0, tag=""):
        return self.op("act", lambda e, _s, _o=out_ap, _i=in_ap, _f=func, _b=bias, _c=scale:
                       e.activation(_o, _i, _f, bias=_b, scale=_c), reads=reads, writes=writes, tag=tag)

    def tt(self, engine, out_ap, a_ap, b_ap, alu, reads, writes, tag=""):
        return self.op(engine, lambda e, _s, _o=out_ap, _a=a_ap, _b=b_ap, _u=alu:
                       e.tensor_tensor(_o, _a, _b, _u), reads=reads, writes=writes, tag=tag)

    def ts(self, engine, out_ap, in_ap, s1, s2, op0, op1, reads, writes, tag=""):
        if s2 is None:
            return self.op(engine, lambda e, _s, _o=out_ap, _i=in_ap, _1=s1, _p0=op0:
                           e.tensor_scalar(_o, _i, _1, None, _p0), reads=reads, writes=writes, tag=tag)
        return self.op(engine, lambda e, _s, _o=out_ap, _i=in_ap, _1=s1, _2=s2, _p0=op0, _p1=op1:
                       e.tensor_scalar(_o, _i, _1, _2, _p0, _p1), reads=reads, writes=writes, tag=tag)

    def stt(self, engine, out_ap, in0_ap, scalar, in1_ap, op0, op1, reads, writes, tag=""):
        return self.op(engine, lambda e, _s, _o=out_ap, _i=in0_ap, _c=scalar, _j=in1_ap, _p0=op0, _p1=op1:
                       e.scalar_tensor_tensor(_o, _i, _c, _j, _p0, _p1), reads=reads, writes=writes, tag=tag)

    def red(self, out_ap, in_ap, alu, reads, writes, negate=False, tag=""):
        return self.op("dve", lambda e, _s, _o=out_ap, _i=in_ap, _u=alu, _n=negate:
                       e.tensor_reduce(_o, _i, AX.X, _u, negate=_n), reads=reads, writes=writes, tag=tag)

    def recip(self, out_ap, in_ap, reads, writes, tag="recip"):
        return self.op("dve", lambda e, _s, _o=out_ap, _i=in_ap: e.reciprocal(_o, _i),
                       reads=reads, writes=writes, tag=tag)

    def vmax8(self, out_ap, in_ap, reads, writes, tag="max8"):
        return self.op("dve", lambda e, _s, _o=out_ap, _i=in_ap: e.max(out=_o, in_=_i),
                       reads=reads, writes=writes, tag=tag)

    def copy(self, engine, out_ap, in_ap, reads, writes, tag="cp"):
        return self.op(engine, lambda e, _s, _o=out_ap, _i=in_ap: e.tensor_copy(_o, _i),
                       reads=reads, writes=writes, tag=tag)

    def memset(self, engine, ap, val, writes, tag="memset"):
        return self.op(engine, lambda e, _s, _a=ap, _v=val: e.memset(_a, _v),
                       reads=(), writes=writes, tag=tag)


class PsumPool:
    def __init__(self, p, name, shape, n=2):
        self.p = p
        self.tiles = [p.palloc(f"{name}{i}", shape) for i in range(n)]
        self.i = 0

    def get(self):
        t = self.tiles[self.i % len(self.tiles)]
        self.i += 1
        return t

    def free(self):
        self.p.free(self.tiles)



def build_nc(parts):
    from contextlib import ExitStack
    nc = bass.Bass("TRN2", target_bir_lowering=False)
    es = ExitStack()

    di = {}

    def inp(name, shape):
        di[name] = nc.dram_tensor(name, list(shape), F32, kind="ExternalInput")
        return di[name]

    XIN = inp("xin", [D, T])
    OUT = nc.dram_tensor("out", [D, T], F32, kind="ExternalOutput")
    XSPILL = nc.dram_tensor("xspill", [D, T], F32, kind="Internal")

    inp("c_misc", [128, 8])
    inp("c_onescol", [128, 1])
    inp("c_onesrow", [1, 640])
    inp("c_onesrow_bf", [1, 640])
    inp("c_idn", [128, 128])
    inp("c_sel4", [4, 512])
    inp("c_sel32", [32, 4096])
    inp("c_ones4", [4, 128])
    inp("c_swap", [128, 128])
    inp("c_cos", [128, T])
    inp("c_sin", [128, T])
    inp("c_dmask", [128, 2048])

    inp("wq", [D, D]); inp("wk", [D, D]); inp("wv", [D, D]); inp("wo", [D, D])
    inp("Wg", [D, 32])
    inp("W1", [32, D, 512]); inp("W3", [32, D, 512]); inp("W2", [32, 512, D])
    inp("Ws1", [D, D]); inp("Ws3", [D, D]); inp("Ws2", [D, D])
    inp("waT", [D, 3 * D]); inp("ba", [1, 3 * D]); inp("wpT", [D, D]); inp("bp", [1, D])
    inp("rw", [4, D, 4]); inp("rb", [1, 16]); inp("rb16", [1, 256])
    inp("gcols", [128, 128]); inp("bcols", [128, 128]); inp("b1cols", [128, 512])
    inp("w1", [4, 4, D, 4096]); inp("w2", [4, 4, 4096, D])
    inp("b2l", [4, 4096])

    p = Prog()
    p.nc = nc
    arena_bytes = (nc.sbuf_bytes_remaining - 1024) // 64 * 64
    p.sb_free = [(0, arena_bytes)]
    p.sb = es.enter_context(nc.sbuf_tensor("arena", [128, arena_bytes], U8))
    p.ps = es.enter_context(nc.psum_tensor("psarena", [128, PSUM_F32], F32))

    CM = p.alloc("c_misc", [128, 8], F32)
    ONESR = p.alloc("c_onesrow", [1, 128], F32)
    ONESRB = p.alloc("c_onesrow_bf", [1, 640], BF16)
    IDN = p.alloc("c_idn", [128, 128], F32)
    SEL4 = p.alloc("c_sel4", [4, 512], BF16)
    ONES4 = p.alloc("c_ones4", [4, 128], BF16)
    GCOLS = p.alloc("gcols", [128, 128], F32)
    BCOLS = p.alloc("bcols", [128, 128], F32)
    B1COLS = p.alloc("b1cols", [128, 512], F32)
    RWB = p.alloc("rw", [128, 128], BF16)
    RB = p.alloc("rb", [1, 16], F32)
    RB16 = p.alloc("rb16", [1, 256], F32)
    B2L = p.alloc("b2l", [4, 4096], BF16)
    WGB = p.alloc("wg", [128, 256], BF16)
    ONESCB = p.alloc("onescb", [128, 1], BF16)

    p.dma("sp", "cst", ONESR, di["c_onesrow"].ap()[:, 0:128])
    p.dma("sp", "cst", RB16, di["rb16"].ap())
    for tile, name in ((CM, "c_misc"), (IDN, "c_idn"),
                       (GCOLS, "gcols"), (BCOLS, "bcols"), (B1COLS, "b1cols"), (RB, "rb")):
        p.dma("sp", "cst", tile, di[name].ap())
    for tile, name in ((ONESRB, "c_onesrow_bf"), (SEL4, "c_sel4"), (ONES4, "c_ones4"), (B2L, "b2l")):
        p.dma("pool", "cstb", tile, di[name].ap())
    p.dma("pool", "cstb", ONESCB, di["c_onescol"].ap())
    p.dma("pool", "cstb", RWB, di["rw"].ap().rearrange("l (c p) e -> p l c e", p=128),
          dst_ap=RWB.ap.rearrange("p (l c e) -> p l c e", l=4, c=8))
    p.dma("pool", "cstb", WGB, di["Wg"].ap().rearrange("(c p) e -> p c e", p=128),
          dst_ap=WGB.ap.rearrange("p (c e) -> p c e", c=8))

    ONES_COL = CM[:, 0:1]
    EPS_S = CM[0:1, 1:2]
    ONESROW128 = ONESR[0:1, 0:128]
    ONESROW512B = ONESRB[0:1, 128:640]
    ONESROW128B = ONESRB[0:1, 0:128]

    x = [p.alloc(f"x{c}", [128, T], F32) for c in range(DC)]
    for c in range(DC):
        p.dma("sp", "xld", x[c], XIN.ap()[128 * c:128 * (c + 1), :])

    xsp = [p.dram_tile(f"xsp{c}", XSPILL.ap()[128 * c:128 * (c + 1), :]) for c in range(DC)]


    def emit_ln(src, dst_dtype):
        dst = [p.alloc(f"ln{i}", [128, T], dst_dtype) for i in range(DC)]
        sq = [p.alloc("lnsq0", [128, T], F32)]
        bf_path = dst_dtype == BF16
        if bf_path:
            sqb = [p.alloc(f"lnqb{i}", [128, T], BF16) for i in range(2)]
        S1 = p.palloc("lnS1", [1, T])
        S2 = p.palloc("lnS2", [1, T])
        for c in range(DC):
            if bf_path:
                xb = dst[c]
                sb = sqb[c % 2]
                p.copy("pool", xb.ap, src[c].ap, reads=[src[c]], writes=[xb], tag="ln_cast")
                p.act(sb.ap, src[c].ap, AF.Square, reads=[src[c]], writes=[sb], tag="ln_sq")
                for sl in range(NSL):
                    p.mm(S1[0:1, SL * sl:SL * (sl + 1)], ONESCB.ap, xb[:, SL * sl:SL * (sl + 1)],
                         start=(c == 0), stop=(c == DC - 1), reads=[ONESCB, xb], writes=[S1], tag="ln_s1")
                    p.mm(S2[0:1, SL * sl:SL * (sl + 1)], ONESCB.ap, sb[:, SL * sl:SL * (sl + 1)],
                         start=(c == 0), stop=(c == DC - 1), reads=[ONESCB, sb], writes=[S2], tag="ln_s2")
            else:
                p.act(sq[0].ap, src[c].ap, AF.Square, reads=[src[c]], writes=[sq[0]], tag="ln_sq")
                for sl in range(NSL):
                    p.mm(S1[0:1, SL * sl:SL * (sl + 1)], ONES_COL, src[c][:, SL * sl:SL * (sl + 1)],
                         start=(c == 0), stop=(c == DC - 1), reads=[CM, src[c]], writes=[S1], tag="ln_s1")
                    p.mm(S2[0:1, SL * sl:SL * (sl + 1)], ONES_COL, sq[0][:, SL * sl:SL * (sl + 1)],
                         start=(c == 0), stop=(c == DC - 1), reads=[CM, sq[0]], writes=[S2], tag="ln_s2")
        rows = p.alloc("lnrows", [1, 3 * T], F32)
        m_row = rows[0:1, 0:T]
        e_row = rows[0:1, T:2 * T]
        w_row = rows[0:1, 2 * T:3 * T]
        r_row = w_row
        p.ts("dve", m_row, S1[0:1, :], 1.0 / D, None, ALU.mult, None, reads=[S1], writes=[rows], tag="ln_m")
        p.ts("dve", e_row, S2[0:1, :], 1.0 / D, None, ALU.mult, None, reads=[S2], writes=[rows], tag="ln_e")
        p.free(S1); p.free(S2)
        p.tt("dve", w_row, m_row, m_row, ALU.mult, reads=[rows], writes=[rows], tag="ln_mm")
        p.tt("dve", w_row, e_row, w_row, ALU.subtract, reads=[rows], writes=[rows], tag="ln_var")
        p.act(e_row, w_row, AF.Ln, reads=[rows, CM], writes=[rows], bias=EPS_S, tag="ln_ln")
        p.act(w_row, e_row, AF.Exp, reads=[rows], writes=[rows], scale=-0.5, tag="ln_rstd")
        p.stt("dve", m_row, m_row, -1.0, r_row, ALU.mult, ALU.mult, reads=[rows], writes=[rows], tag="ln_b")
        A = p.alloc("lnA", [128, T], F32)
        Bb = p.alloc("lnB", [128, T], F32)
        PA = p.palloc("lnPA", [128, T])
        for sl in range(NSL):
            p.mm(PA[:, SL * sl:SL * (sl + 1)], ONESROW128, r_row[0:1, SL * sl:SL * (sl + 1)],
                 start=True, stop=True, reads=[ONESR, rows], writes=[PA], tag="ln_bcA")
        p.copy("dve", A.ap, PA.ap, reads=[PA], writes=[A], tag="ln_evA")
        for sl in range(NSL):
            p.mm(PA[:, SL * sl:SL * (sl + 1)], ONESROW128, m_row[0:1, SL * sl:SL * (sl + 1)],
                 start=True, stop=True, reads=[ONESR, rows], writes=[PA], tag="ln_bcB")
        p.copy("dve", Bb.ap, PA.ap, reads=[PA], writes=[Bb], tag="ln_evB")
        p.free(PA)
        for c in range(DC):
            u = sq[0]
            p.tt("dve", u.ap, src[c].ap, A.ap, ALU.mult, reads=[src[c], A], writes=[u], tag="ln_u")
            p.tt("pool", dst[c].ap, u.ap, Bb.ap, ALU.add, reads=[u, Bb], writes=[dst[c]], tag="ln_o")
        p.free(rows); p.free(A); p.free(Bb); p.free(sq)
        if bf_path:
            p.free(sqb)
        return dst

    def emit_l0_ffn():
        p.phase += 1
        ph = p.phase
        xln = emit_ln(x, BF16)
        SEL32 = p.alloc("sel32", [32, 4096], BF16)
        p.dma("pool", f"cb{ph}", SEL32, di["c_sel32"].ap())
        GFM = p.alloc("gfm", [32, T], BF16)
        ex = p.alloc("rt_ex", [128, 512], F32)
        ex2 = p.alloc("rt_ex2", [128, 512], F32)
        exp_ = p.alloc("rt_exp", [128, 512], F32)
        scr = p.alloc("rt_nm", [128, 96], F32)
        PL = p.palloc("rt_ps", [128, 512])
        n = 0
        for tc in range(NTC):
            for c in range(DC):
                n += 1
                p.mm(PL[:, 32 * tc:32 * (tc + 1)], xln[c][:, 128 * tc:128 * (tc + 1)],
                     WGB[:, 32 * c:32 * (c + 1)],
                     start=(n == 1), stop=(n == NTC * DC), reads=[xln[c], WGB], writes=[PL], tag="rt_mm")
        PLv = PL.ap.rearrange("p (g e) -> p g e", e=32)
        nm_b = Prog.bcast_ap(scr, 0, NTC, 32)
        p.red(scr[:, 0:16], PLv, ALU.max, reads=[PL], writes=[scr], negate=True, tag="rt_max")
        p.tt("dve", ex.ap.rearrange("p (g e) -> p g e", e=32), PLv, nm_b, ALU.add,
             reads=[PL, scr], writes=[ex], tag="rt_sub")
        p.free(PL)
        p.act(exp_.ap, ex.ap, AF.Exp, reads=[ex], writes=[exp_], tag="rt_exp")
        expv = exp_.ap.rearrange("p (g e) -> p g e", e=32)
        cur = exp_
        for it in range(4):
            p.red(scr[:, 16 * (it + 2):16 * (it + 3)] if False else scr[:, 16:32], 
                  cur.ap.rearrange("p (g e) -> p g e", e=32), ALU.max,
                  reads=[cur], writes=[scr], tag="rt_it_max")
            if it < 3:
                mb = Prog.bcast_ap(scr, 16, NTC, 32)
                nxt = ex2 if cur is not ex2 else ex
                p.tt("dve", nxt.ap.rearrange("p (g e) -> p g e", e=32),
                     cur.ap.rearrange("p (g e) -> p g e", e=32), mb, ALU.is_lt,
                     reads=[cur, scr], writes=[nxt], tag="rt_it_ge")
                p.stt("dve", nxt.ap, nxt.ap, 1e30, cur.ap, ALU.mult, ALU.min,
                      reads=[cur, nxt], writes=[nxt], tag="rt_it_mask")
                cur = nxt
        th_b = Prog.bcast_ap(scr, 16, NTC, 32)
        p.tt("dve", ex2.ap.rearrange("p (g e) -> p g e", e=32) if cur is not ex2 else ex.ap.rearrange("p (g e) -> p g e", e=32),
             expv, th_b, ALU.is_ge, reads=[exp_, scr], writes=[ex2 if cur is not ex2 else ex], tag="rt_ge4")
        sel = ex2 if cur is not ex2 else ex
        p.tt("dve", sel.ap, sel.ap, exp_.ap, ALU.mult, reads=[sel, exp_], writes=[sel], tag="rt_gate0")
        p.red(scr[:, 32:48], expv, ALU.add, reads=[exp_], writes=[scr], tag="rt_sum")
        p.recip(scr[:, 48:64], scr[:, 32:48], reads=[scr], writes=[scr], tag="rt_rec")
        r_b = Prog.bcast_ap(scr, 48, NTC, 32)
        p.tt("dve", sel.ap.rearrange("p (g e) -> p g e", e=32), sel.ap.rearrange("p (g e) -> p g e", e=32),
             r_b, ALU.mult, reads=[sel, scr], writes=[sel], tag="rt_gate")
        PTp = PsumPool(p, "rt_pt", [32, 128], 2)
        for tc in range(NTC):
            PT = PTp.get()
            p.transp(PT.ap, sel[:, 32 * tc:32 * (tc + 1)], IDN.ap, reads=[sel, IDN], writes=[PT], tag="rt_tr")
            p.copy("dve", GFM[:, 128 * tc:128 * (tc + 1)], PT.ap, reads=[PT], writes=[GFM], tag="rt_ev")
        PTp.free()
        p.free(ex); p.free(ex2); p.free(exp_); p.free(scr)

        wslot = []
        for s in range(2):
            wslot.append(dict(
                w1=p.alloc(f"l0w1_{s}", [128, 8 * 512], BF16),
                w3=p.alloc(f"l0w3_{s}", [128, 8 * 512], BF16),
                w2=p.alloc(f"l0w2_{s}", [128, 4 * 1024], BF16),
            ))
        hg = [p.alloc(f"l0hg{i}", [128, T], BF16) for i in range(4)]
        Bsb = p.alloc("l0be", [128, T], F32)
        s1t = [p.alloc(f"l0s1_{i}", [128, SL], BF16) for i in range(2)]
        tbt = [p.alloc(f"l0tb_{i}", [128, SL], BF16) for i in range(2)]

        def load_l0w(e, s):
            w = wslot[s]
            if e < 32:
                srcs = (di["W1"].ap()[e], di["W3"].ap()[e], di["W2"].ap()[e])
            else:
                hb = e - 32
                srcs = (di["Ws1"].ap()[:, 512 * hb:512 * (hb + 1)],
                        di["Ws3"].ap()[:, 512 * hb:512 * (hb + 1)],
                        di["Ws2"].ap()[512 * hb:512 * (hb + 1), :])
            p.dma("pool", f"l0a{s}", w["w1"], srcs[0].rearrange("(c p) h -> p c h", p=128),
                  dst_ap=w["w1"].ap.rearrange("p (c h) -> p c h", c=8), bulk=False)
            p.dma("pool", f"l0b{s}", w["w3"], srcs[1].rearrange("(c p) h -> p c h", p=128),
                  dst_ap=w["w3"].ap.rearrange("p (c h) -> p c h", c=8), bulk=False)
            p.dma("pool", f"l0c{s}", w["w2"], srcs[2].rearrange("(c p) h -> p c h", p=128),
                  dst_ap=w["w2"].ap.rearrange("p (c h) -> p c h", c=4), bulk=False)

        P1p = PsumPool(p, "l0ps", [128, SL], 6)
        P3p = P1p
        PYp = P1p
        PBp = PsumPool(p, "l0pb", [128, SL], 1)

        load_l0w(0, 0)
        for e in range(34):
            s = e % 2
            w = wslot[s]
            if e + 1 < 34:
                load_l0w(e + 1, (e + 1) % 2)
            gated = e < 32
            if gated:
                for sl in range(NSL):
                    PB = PBp.get()
                    p.mm(PB.ap, SEL32[:, 128 * e:128 * (e + 1)], GFM[:, SL * sl:SL * (sl + 1)],
                         start=True, stop=True, reads=[SEL32, GFM], writes=[PB], tag="l0_bc")
                    p.copy("dve", Bsb[:, SL * sl:SL * (sl + 1)], PB.ap, reads=[PB], writes=[Bsb], tag="l0_bcev")
            w1v = w["w1"].ap.rearrange("p (c h) -> p c h", c=8)
            w3v = w["w3"].ap.rearrange("p (c h) -> p c h", c=8)
            w2v = w["w2"].ap.rearrange("p (c h) -> p c h", c=4)
            for hc in range(4):
                for sl in range(NSL):
                    P1 = P1p.get()
                    for c in range(DC):
                        p.mm(P1.ap, w1v[:, c, 128 * hc:128 * (hc + 1)], xln[c][:, SL * sl:SL * (sl + 1)],
                             start=(c == 0), stop=(c == DC - 1), reads=[w["w1"], xln[c]], writes=[P1], tag="l0_w1")
                    st = s1t[sl % 2]
                    p.act(st.ap, P1.ap, AF.Silu, reads=[P1], writes=[st], tag="l0_silu")
                    P3 = P3p.get()
                    for c in range(DC):
                        p.mm(P3.ap, w3v[:, c, 128 * hc:128 * (hc + 1)], xln[c][:, SL * sl:SL * (sl + 1)],
                             start=(c == 0), stop=(c == DC - 1), reads=[w["w3"], xln[c]], writes=[P3], tag="l0_w3")
                    tb = tbt[sl % 2]
                    if gated:
                        p.tt("dve", tb.ap, P3.ap, Bsb[:, SL * sl:SL * (sl + 1)], ALU.mult,
                             reads=[P3, Bsb], writes=[tb], tag="l0_tb")
                    else:
                        p.copy("dve", tb.ap, P3.ap, reads=[P3], writes=[tb], tag="l0_tb")
                    p.tt("pool", hg[hc][:, SL * sl:SL * (sl + 1)], st.ap, tb.ap, ALU.mult,
                         reads=[st, tb], writes=[hg[hc]], tag="l0_hg")
            for dc in range(DC):
                for sl in range(NSL):
                    PY = PYp.get()
                    for hc in range(4):
                        p.mm(PY.ap, w2v[:, hc, 128 * dc:128 * (dc + 1)], hg[hc][:, SL * sl:SL * (sl + 1)],
                             start=(hc == 0), stop=(hc == 3), reads=[w["w2"], hg[hc]], writes=[PY], tag="l0_w2")
                    p.stt("dve", x[dc][:, SL * sl:SL * (sl + 1)], PY.ap, 0.5, x[dc][:, SL * sl:SL * (sl + 1)],
                          ALU.mult, ALU.add, reads=[PY, x[dc]], writes=[x[dc]], tag="l0_acc")
        P1p.free(); PBp.free()
        p.free(xln); p.free(SEL32); p.free(GFM); p.free(hg); p.free(Bsb); p.free(s1t); p.free(tbt)
        for s in range(2):
            p.free(wslot[s])

    def emit_attn(layer):
        p.phase += 1
        ph = p.phase
        xln = emit_ln(x, BF16)
        for c in range(DC):
            p.dma("sp", f"sp{ph}", xsp[c], x[c])
        for c in range(DC):
            p.free(x[c])

        DMK = p.alloc("dmask", [128, 2048], F32)
        p.dma("sp", f"cs{ph}", DMK, di["c_dmask"].ap())
        if layer == 0:
            COS = p.alloc("cos", [128, T], BF16)
            SIN = p.alloc("sin", [128, T], BF16)
            SWP = p.alloc("swap", [128, 128], BF16)
            p.dma("pool", f"cb{ph}", COS, di["c_cos"].ap())
            p.dma("pool", f"cb{ph}", SIN, di["c_sin"].ap())
            p.dma("pool", f"cb{ph}", SWP, di["c_swap"].ap())
        else:
            BA = p.alloc("ba", [1, 3 * D], BF16)
            p.dma("pool", f"cb{ph}", BA, di["ba"].ap())

        q_b = [p.alloc(f"qb{c}", [128, T], BF16) for c in range(DC)]
        k_b = [p.alloc(f"kb{c}", [128, T], BF16) for c in range(DC)]
        vbuf = [p.alloc(f"vb{tc}", [128, 16 * 65], BF16) for tc in range(NTC)]
        for tc in range(NTC):
            p.memset("pool", vbuf[tc].ap.rearrange("p (h n) -> p h n", n=65)[:, :, 64:65], 1.0,
                     writes=[vbuf[tc]], tag="v_ones")

        wsl = [p.alloc(f"aw{s}", [128, 8 * 1024], BF16) for s in range(2)]

        def load_w(name, s, colslice=None):
            if layer == 0:
                src = di[name].ap()
            else:
                src = di["waT"].ap()[:, colslice] if colslice is not None else di[name].ap()
            p.dma("pool", f"aw{ph}{s}", wsl[s], src.rearrange("(c p) n -> p c n", p=128),
                  dst_ap=wsl[s].ap.rearrange("p (c n) -> p c n", c=8), bulk=False)

        if layer == 0:
            load_w("wq", 0)
            load_w("wk", 1)
        else:
            load_w(None, 0, colslice=slice(0, D))
            load_w(None, 1, colslice=slice(D, 2 * D))

        PQp = PsumPool(p, "a_pq", [128, SL], 4)

        def proj_fm(dsts, s, bias_off):
            wv_ = wsl[s].ap.rearrange("p (c n) -> p c n", c=8)
            for jc in range(DC):
                for sl in range(NSL):
                    PQ = PQp.get()
                    for c in range(DC):
                        p.mm(PQ.ap, wv_[:, c, 128 * jc:128 * (jc + 1)], xln[c][:, SL * sl:SL * (sl + 1)],
                             start=(c == 0), stop=(layer == 0 and c == DC - 1),
                             reads=[wsl[s], xln[c]], writes=[PQ], tag="a_proj")
                    if layer == 1:
                        p.mm(PQ.ap, BA[0:1, bias_off + 128 * jc: bias_off + 128 * (jc + 1)],
                             ONESROW512B, start=False, stop=True, reads=[BA, ONESRB], writes=[PQ], tag="a_bias")
                    p.copy("dve", dsts[jc][:, SL * sl:SL * (sl + 1)], PQ.ap, reads=[PQ], writes=[dsts[jc]], tag="a_qev")

        if layer == 0:
            proj_fm(q_b, 0, 0)
            load_w("wv", 0)
            proj_fm(k_b, 1, 0)
        else:
            proj_fm(q_b, 0, 0)
            load_w(None, 0, colslice=slice(2 * D, 3 * D))
            proj_fm(k_b, 1, D)

        wv_ = wsl[0].ap.rearrange("p (c n) -> p c n", c=8)
        for tc in range(NTC):
            for half in range(2):
                PV_ = PQp.get()
                for c in range(DC):
                    p.mm(PV_.ap, xln[c][:, 128 * tc:128 * (tc + 1)], wv_[:, c, SL * half:SL * (half + 1)],
                         start=(c == 0), stop=(layer == 0 and c == DC - 1),
                         reads=[wsl[0], xln[c]], writes=[PV_], tag="a_v")
                if layer == 1:
                    p.mm(PV_.ap, ONESROW128B, BA[0:1, 2 * D + SL * half: 2 * D + SL * (half + 1)],
                         start=False, stop=True, reads=[ONESRB, BA], writes=[PV_], tag="a_vb")
                p.copy("dve", vbuf[tc].ap.rearrange("p (h n) -> p h n", n=65)[:, 8 * half:8 * (half + 1), 0:64],
                       PV_.ap.rearrange("p (h n) -> p h n", n=64), reads=[PV_], writes=[vbuf[tc]], tag="a_vev")

        if layer == 0:
            ur = p.alloc("rope_u", [128, SL], F32)
            t1r = p.alloc("rope_t1", [128, SL], F32)
            PSp = PsumPool(p, "a_sw", [128, SL], 4)
            for tiles in (q_b, k_b):
                for jc in range(DC):
                    for sl in range(NSL):
                        PS = PSp.get()
                        p.mm(PS.ap, SWP.ap, tiles[jc][:, SL * sl:SL * (sl + 1)], start=True, stop=True,
                             reads=[SWP, tiles[jc]], writes=[PS], tag="rope_mm")
                        p.tt("dve", ur.ap, PS.ap, SIN[:, SL * sl:SL * (sl + 1)], ALU.mult,
                             reads=[PS, SIN], writes=[ur], tag="rope_u")
                        p.tt("pool", t1r.ap, tiles[jc][:, SL * sl:SL * (sl + 1)], COS[:, SL * sl:SL * (sl + 1)],
                             ALU.mult, reads=[tiles[jc], COS], writes=[t1r], tag="rope_t1")
                        p.tt("dve", tiles[jc][:, SL * sl:SL * (sl + 1)], t1r.ap, ur.ap, ALU.add,
                             reads=[t1r, ur], writes=[tiles[jc]], tag="rope_o")
            PSp.free()
            p.free(ur); p.free(t1r)
            p.free(COS); p.free(SIN); p.free(SWP)
        PQp.free()

        p.free(xln)
        p.free(wsl)
        if layer == 1:
            p.free(BA)
        o_all = [p.alloc(f"oa{c}", [128, T], BF16) for c in range(DC)]
        pT = [p.alloc(f"pt{i}", [128, SL], BF16) for i in range(4)]
        r_row = p.alloc("a_rrow", [1, T], F32)
        o_raw = p.alloc("a_oraw", [64, T], BF16)
        rb_sb = p.alloc("a_rbc", [64, T], BF16)

        pti = 0
        for h in range(16):
            kc = h // 2
            base = 64 * (h % 2)
            PVp = p.palloc("a_pvps", [65, T])
            STpool = PsumPool(p, "a_st", [128, SL], 3)
            slices = []
            for c in range(NTC):
                for qs in range(c // 4, NSL):
                    slices.append((c, qs))
            pend = []
            for i in range(len(slices) + 2):
                if i < len(slices):
                    c, qs = slices[i]
                    ST = STpool.get()
                    p.mm(ST.ap,
                         k_b[kc][base:base + 64, 128 * c:128 * (c + 1)],
                         q_b[kc][base:base + 64, SL * qs:SL * (qs + 1)],
                         start=True, stop=True, reads=[k_b[kc], q_b[kc]], writes=[ST], tag="a_score")
                    if qs == c // 4:
                        p.tt("dve", ST.ap, ST.ap, DMK[:, SL * (c % 4):SL * (c % 4 + 1)],
                             ALU.add, reads=[DMK, ST], writes=[ST], tag="a_mask")
                    pt = pT[pti % 4]
                    pti += 1
                    p.act(pt.ap, ST.ap, AF.Exp, reads=[ST], writes=[pt], scale=0.125, tag="a_exp")
                    pend.append((c, qs, pt))
                if len(pend) > 2 or (i >= len(slices) and pend):
                    c2, qs2, pt2 = pend.pop(0)
                    p.mm(PVp[0:65, SL * qs2:SL * (qs2 + 1)],
                         vbuf[c2].ap.rearrange("p (hh n) -> p hh n", n=65)[:, h, :],
                         pt2.ap,
                         start=(c2 == 0), stop=(c2 == 4 * qs2 + 3), reads=[vbuf[c2], pt2], writes=[PVp], tag="a_pv")
            p.recip(r_row.ap, PVp[64:65, :], reads=[PVp], writes=[r_row], tag="a_recip")
            p.act(o_raw.ap, PVp[0:64, :], AF.Copy, reads=[PVp], writes=[o_raw], tag="a_oraw")
            STpool.free()
            RBC = p.palloc("a_rbcp", [64, T])
            for sl in range(NSL):
                p.mm(RBC[:, SL * sl:SL * (sl + 1)], ONESR[0:1, 0:64], r_row[0:1, SL * sl:SL * (sl + 1)],
                     start=True, stop=True, reads=[ONESR, r_row], writes=[RBC], tag="a_rbcmm")
            p.copy("dve", rb_sb.ap, RBC.ap, reads=[RBC], writes=[rb_sb], tag="a_rbcev")
            p.free(RBC)
            p.tt("pool", o_all[kc][base:base + 64, :], o_raw.ap, rb_sb.ap, ALU.mult,
                 reads=[o_raw, rb_sb], writes=[o_all[kc]], tag="a_onorm")
            p.free(PVp)

        p.free(q_b); p.free(k_b); p.free(vbuf); p.free(pT)
        p.free(r_row); p.free(o_raw); p.free(rb_sb); p.free(DMK)

        wo = p.alloc("awo", [128, 8 * 1024], BF16)
        p.dma("pool", f"awo{ph}", wo, di["wo" if layer == 0 else "wpT"].ap().rearrange("(c p) n -> p c n", p=128),
              dst_ap=wo.ap.rearrange("p (c n) -> p c n", c=8), bulk=False)
        BP = None
        if layer == 1:
            BP = p.alloc("bp", [1, D], BF16)
            p.dma("pool", f"cb2{ph}", BP, di["bp"].ap())
        xn = [p.alloc(f"xn{c}", [128, T], F32) for c in range(DC)]
        for c in range(DC):
            p.dma("sp", f"xr{ph}", xn[c], xsp[c])
        wov = wo.ap.rearrange("p (c n) -> p c n", c=8)
        POp = PsumPool(p, "a_po", [128, SL], 4)
        for dc in range(DC):
            for sl in range(NSL):
                PO = POp.get()
                for c in range(DC):
                    p.mm(PO.ap, wov[:, c, 128 * dc:128 * (dc + 1)], o_all[c][:, SL * sl:SL * (sl + 1)],
                         start=(c == 0), stop=(layer == 0 and c == DC - 1),
                         reads=[wo, o_all[c]], writes=[PO], tag="a_wo")
                if layer == 1:
                    p.mm(PO.ap, BP[0:1, 128 * dc:128 * (dc + 1)], ONESROW512B,
                         start=False, stop=True, reads=[BP, ONESRB], writes=[PO], tag="a_wob")
                p.tt("dve", xn[dc][:, SL * sl:SL * (sl + 1)], PO.ap, xn[dc][:, SL * sl:SL * (sl + 1)],
                     ALU.add, reads=[PO, xn[dc]], writes=[xn[dc]], tag="a_res")
        POp.free()
        p.free(o_all); p.free(wo)
        if BP is not None:
            p.free(BP)
        for c in range(DC):
            x[c] = xn[c]

    def emit_l1_ffn():
        p.phase += 1
        ph = p.phase
        t = emit_ln(x, F32)
        for c in range(DC):
            p.dma("sp", f"sp{ph}", xsp[c], x[c])
        for c in range(DC):
            p.free(x[c])

        CFM = p.alloc("cfm", [4, T], BF16)
        for l in range(4):
            tb = [p.alloc(f"l1tb{c}", [128, T], BF16) for c in range(DC)]
            for c in range(DC):
                p.copy("pool", tb[c].ap, t[c].ap, reads=[t[c]], writes=[tb[c]], tag="l1_tb")
            ex = p.alloc("l1_ex", [128, 64], F32)
            ex2 = p.alloc("l1_ex2", [128, 64], F32)
            exp_ = p.alloc("l1_exp", [128, 64], F32)
            scr = p.alloc("l1_scr", [128, 96], F32)
            PL = p.palloc("l1_psl", [128, 64])
            n = 0
            for tc in range(NTC):
                for c in range(DC):
                    n += 1
                    p.mm(PL[:, 4 * tc:4 * (tc + 1)], tb[c][:, 128 * tc:128 * (tc + 1)],
                         RWB[:, 32 * l + 4 * c:32 * l + 4 * (c + 1)],
                         start=(n == 1), stop=False, reads=[tb[c], RWB], writes=[PL], tag="l1_rt")
            p.mm(PL.ap, ONESROW128, RB16[0:1, 64 * l:64 * (l + 1)], start=False, stop=True,
                 reads=[ONESR, RB16], writes=[PL], tag="l1_rtb")
            PLv = PL.ap.rearrange("p (g e) -> p g e", e=4)
            p.red(scr[:, 0:16], PLv, ALU.max, reads=[PL], writes=[scr], negate=True, tag="l1_max")
            nm_b = Prog.bcast_ap(scr, 0, NTC, 4)
            p.tt("dve", ex.ap.rearrange("p (g e) -> p g e", e=4), PLv, nm_b, ALU.add,
                 reads=[PL, scr], writes=[ex], tag="l1_sub")
            p.free(PL)
            p.act(exp_.ap, ex.ap, AF.Exp, reads=[ex], writes=[exp_], tag="l1_exp")
            expv = exp_.ap.rearrange("p (g e) -> p g e", e=4)
            p.red(scr[:, 16:32], expv, ALU.max, reads=[exp_], writes=[scr], tag="l1_m1")
            m_b = Prog.bcast_ap(scr, 16, NTC, 4)
            p.tt("dve", ex2.ap.rearrange("p (g e) -> p g e", e=4), expv, m_b, ALU.is_lt,
                 reads=[exp_, scr], writes=[ex2], tag="l1_lt")
            p.stt("dve", ex2.ap, ex2.ap, 1e30, exp_.ap, ALU.mult, ALU.min,
                  reads=[exp_, ex2], writes=[ex2], tag="l1_mask1")
            p.red(scr[:, 16:32], ex2.ap.rearrange("p (g e) -> p g e", e=4), ALU.max,
                  reads=[ex2], writes=[scr], tag="l1_m2")
            th_b = Prog.bcast_ap(scr, 16, NTC, 4)
            p.tt("dve", ex2.ap.rearrange("p (g e) -> p g e", e=4), expv, th_b, ALU.is_ge,
                 reads=[exp_, scr], writes=[ex2], tag="l1_ge2")
            p.tt("dve", ex2.ap, ex2.ap, exp_.ap, ALU.mult, reads=[ex2, exp_], writes=[ex2], tag="l1_vals")
            p.red(scr[:, 32:48], ex2.ap.rearrange("p (g e) -> p g e", e=4), ALU.add,
                  reads=[ex2], writes=[scr], tag="l1_vsum")
            p.recip(scr[:, 48:64], scr[:, 32:48], reads=[scr], writes=[scr], tag="l1_vrec")
            r_b = Prog.bcast_ap(scr, 48, NTC, 4)
            p.tt("dve", ex2.ap.rearrange("p (g e) -> p g e", e=4), ex2.ap.rearrange("p (g e) -> p g e", e=4),
                 r_b, ALU.mult, reads=[ex2, scr], writes=[ex2], tag="l1_cnorm")
            PTp = PsumPool(p, "l1_ct", [4, 128], 2)
            for tc in range(NTC):
                PT = PTp.get()
                p.transp(PT.ap, ex2[:, 4 * tc:4 * (tc + 1)], IDN.ap, reads=[ex2, IDN], writes=[PT], tag="l1_ctr")
                p.copy("dve", CFM[:, 128 * tc:128 * (tc + 1)], PT.ap, reads=[PT], writes=[CFM], tag="l1_cev")
            PTp.free()
            p.free(tb); p.free(ex); p.free(ex2); p.free(exp_); p.free(scr)

            xhat = emit_ln(t, BF16)

            Ssb = p.alloc("l1_S", [128, T], F32)
            PSp = PsumPool(p, "l1_ps", [128, SL], 3)
            for sl in range(NSL):
                PS_ = PSp.get()
                p.mm(PS_.ap, ONES4.ap, CFM[:, SL * sl:SL * (sl + 1)], start=True, stop=True,
                     reads=[ONES4, CFM], writes=[PS_], tag="l1_smm")
                p.copy("dve", Ssb[:, SL * sl:SL * (sl + 1)], PS_.ap, reads=[PS_], writes=[Ssb], tag="l1_sev")
            for dc in range(DC):
                for sl in range(NSL):
                    p.tt("pool", t[dc][:, SL * sl:SL * (sl + 1)], t[dc][:, SL * sl:SL * (sl + 1)],
                         Ssb[:, SL * sl:SL * (sl + 1)], ALU.mult, reads=[Ssb, t[dc]], writes=[t[dc]], tag="l1_sc")
            p.free(Ssb)
            for dc in range(DC):
                for sl in range(NSL):
                    PS_ = PSp.get()
                    p.mm(PS_.ap, B2L[:, 1024 * l + 128 * dc:1024 * l + 128 * (dc + 1)],
                         CFM[:, SL * sl:SL * (sl + 1)], start=True, stop=True,
                         reads=[B2L, CFM], writes=[PS_], tag="l1_b2")
                    p.tt("dve", t[dc][:, SL * sl:SL * (sl + 1)], PS_.ap, t[dc][:, SL * sl:SL * (sl + 1)],
                         ALU.add, reads=[PS_, t[dc]], writes=[t[dc]], tag="l1_b2a")
            PSp.free()

            w1s = [p.alloc(f"l1w1_{s}", [128, 8 * 1024], BF16) for s in range(2)]
            w2s = [p.alloc("l1w2_0", [128, 8 * 1024], BF16)]
            Bsb = p.alloc("l1_be", [128, T], BF16)
            xet = [p.alloc(f"l1xe{i}", [128, 8 * SL], BF16) for i in range(2)]
            hts = [p.alloc(f"l1h{i}", [128, SL], BF16) for i in range(8)]
            PHp = PsumPool(p, "l1_ph", [128, SL], 3)
            PYp = PsumPool(p, "l1_py", [128, SL], 3)
            PBp = PsumPool(p, "l1_pb", [128, SL], 1)

            def load_l1w1(e, fb, s):
                p.dma("pool", f"l1a{s}", w1s[s],
                      di["w1"].ap()[l, e][:, 1024 * fb:1024 * (fb + 1)].rearrange("(c p) f -> p c f", p=128),
                      dst_ap=w1s[s].ap.rearrange("p (c f) -> p c f", c=8), bulk=False)

            def load_l1w2(e, fb, sl_):
                sl_ = 0
                p.dma("pool", "l1b0", w2s[0],
                      di["w2"].ap()[l, e][1024 * fb:1024 * (fb + 1), :].rearrange("(c p) h -> p c h", p=128),
                      dst_ap=w2s[sl_].ap.rearrange("p (c h) -> p c h", c=8), bulk=False)

            load_l1w1(0, 0, 0)
            load_l1w2(0, 0, 0)
            step = 0
            for e in range(4):
                for sl in range(NSL):
                    PB = PBp.get()
                    p.mm(PB.ap, SEL4[:, 128 * e:128 * (e + 1)], CFM[:, SL * sl:SL * (sl + 1)],
                         start=True, stop=True, reads=[SEL4, CFM], writes=[PB], tag="l1_bc")
                    p.copy("dve", Bsb[:, SL * sl:SL * (sl + 1)], PB.ap, reads=[PB], writes=[Bsb], tag="l1_bcev")
                def emit_xe(xbuf, ee, tc):
                    xev = xbuf.ap.rearrange("p (c n) -> p c n", c=8)
                    for c in range(DC):
                        p.ts("dve", xev[:, c, :], xhat[c][:, SL * tc:SL * (tc + 1)],
                             GCOLS[:, 32 * l + 8 * ee + c:32 * l + 8 * ee + c + 1],
                             BCOLS[:, 32 * l + 8 * ee + c:32 * l + 8 * ee + c + 1],
                             ALU.mult, ALU.add, reads=[xhat[c], GCOLS, BCOLS], writes=[xbuf], tag="l1_xe")

                for fb in range(4):
                    s = step % 2
                    if step + 1 < 16:
                        e2, fb2 = (e, fb + 1) if fb + 1 < 4 else (e + 1, 0)
                        load_l1w1(e2, fb2, (step + 1) % 2)
                    w1v = w1s[s].ap.rearrange("p (c f) -> p c f", c=8)
                    w2v = w2s[0].ap.rearrange("p (c h) -> p c h", c=8)
                    if e == 0 and fb == 0:
                        emit_xe(xet[0], e, 0)
                    for tc in range(NSL):
                        xe = xet[tc % 2]
                        xev = xe.ap.rearrange("p (c n) -> p c n", c=8)
                        for fc in range(8):
                            PH = PHp.get()
                            for c in range(DC):
                                p.mm(PH.ap, w1v[:, c, 128 * fc:128 * (fc + 1)], xev[:, c, :],
                                     start=(c == 0), stop=(c == DC - 1), reads=[w1s[s], xe], writes=[PH], tag="l1_w1")
                            p.act(hts[fc].ap, PH.ap, AF.Gelu,
                                  reads=[PH, B1COLS], writes=[hts[fc]],
                                  bias=B1COLS[:, 128 * l + 32 * e + 8 * fb + fc:128 * l + 32 * e + 8 * fb + fc + 1],
                                  tag="l1_gelu")
                            p.tt("pool", hts[fc].ap, hts[fc].ap, Bsb[:, SL * tc:SL * (tc + 1)], ALU.mult,
                                 reads=[Bsb, hts[fc]], writes=[hts[fc]], tag="l1_hg")
                        if tc + 1 < NSL:
                            emit_xe(xet[(tc + 1) % 2], e, tc + 1)
                        elif step + 1 < 16:
                            emit_xe(xet[0], e2, 0)
                        for dc in range(DC):
                            PY = PYp.get()
                            for fc in range(8):
                                p.mm(PY.ap, w2v[:, fc, 128 * dc:128 * (dc + 1)], hts[fc].ap,
                                     start=(fc == 0), stop=(fc == 7), reads=[w2s[0], hts[fc]], writes=[PY], tag="l1_w2")
                            p.tt("dve", t[dc][:, SL * tc:SL * (tc + 1)], PY.ap, t[dc][:, SL * tc:SL * (tc + 1)],
                                 ALU.add, reads=[PY, t[dc]], writes=[t[dc]], tag="l1_acc")
                    if step + 1 < 16:
                        load_l1w2(e2, fb2, 0)
                    step += 1
            PHp.free(); PYp.free(); PBp.free()
            p.free(w1s); p.free(w2s); p.free(Bsb); p.free(xet); p.free(hts); p.free(xhat)
        p.free(CFM)
        xn = [p.alloc(f"xf{c}", [128, T], F32) for c in range(DC)]
        for c in range(DC):
            p.dma("sp", f"xr{ph}", xn[c], xsp[c])
        for c in range(DC):
            p.stt("dve", xn[c].ap, t[c].ap, 0.5, xn[c].ap, ALU.mult, ALU.add,
                  reads=[t[c], xn[c]], writes=[xn[c]], tag="l1_fin")
        p.free(t)
        for c in range(DC):
            x[c] = xn[c]

    for part in parts:
        if part == "l0ffn":
            emit_l0_ffn()
        elif part == "l0attn":
            emit_attn(0)
        elif part == "l1attn":
            emit_attn(1)
        elif part == "l1ffn":
            emit_l1_ffn()
        else:
            raise ValueError(part)

    for c in range(DC):
        ot = p.dram_tile(f"out{c}", OUT.ap()[128 * c:128 * (c + 1), :])
        p.dma("sp", "ost", ot, x[c])
    n_out = p.dma_count["ost"] * 16
    p.op("sp", lambda e, _s: e.wait_ge(_s["ost"], n_out), reads=(), writes=(), tag="final")

    p.emit(es, nc)
    return nc, es, p



def _consts():
    c = {}
    cm = np.zeros((128, 8), np.float32)
    cm[:, 0] = 1.0
    cm[:, 1] = 1e-5
    c["c_misc"] = cm
    c["c_onescol"] = np.ones((128, 1), np.float32)
    c["c_onesrow"] = np.ones((1, 640), np.float32)
    c["c_onesrow_bf"] = np.ones((1, 640), np.float32)
    c["c_idn"] = np.eye(128, dtype=np.float32)
    sel4 = np.zeros((4, 512), np.float32)
    for e in range(4):
        sel4[e, 128 * e:128 * (e + 1)] = 1.0
    c["c_sel4"] = sel4
    sel32 = np.zeros((32, 4096), np.float32)
    for e in range(32):
        sel32[e, 128 * e:128 * (e + 1)] = 1.0
    c["c_sel32"] = sel32
    c["c_ones4"] = np.ones((4, 128), np.float32)
    swap = np.zeros((128, 128), np.float32)
    for i in range(0, 128, 2):
        swap[i + 1, i] = 1.0
        swap[i, i + 1] = 1.0
    c["c_swap"] = swap
    hd = 64
    inv = 1.0 / (10000.0 ** (np.arange(0, hd, 2, dtype=np.float64) / hd))
    pos = np.arange(T, dtype=np.float64)
    cos = np.zeros((128, T), np.float64)
    sin = np.zeros((128, T), np.float64)
    for pp in range(128):
        f = inv[(pp % 64) // 2]
        cos[pp] = np.cos(pos * f)
        sgn = -1.0 if pp % 2 == 0 else 1.0
        sin[pp] = sgn * np.sin(pos * f)
    c["c_cos"] = cos.astype(np.float32)
    c["c_sin"] = sin.astype(np.float32)
    dm = np.zeros((128, 2048), np.float32)
    for m in range(4):
        k = np.arange(128)[:, None]
        j = np.arange(512)[None, :]
        dm[:, 512 * m:512 * (m + 1)] = np.where(128 * m + k <= j, 0.0, NEG_BIG)
    c["c_dmask"] = dm
    return c


def host_inputs(A):
    shared = _consts()
    shared.update(dict(
        wq=A["L0_attn_wq"], wk=A["L0_attn_wk"], wv=A["L0_attn_wv"], wo=A["L0_attn_wo"],
        Wg=A["L0_ffn_Wg"], W1=A["L0_ffn_W1"], W3=A["L0_ffn_W3"], W2=A["L0_ffn_W2"],
        Ws1=A["L0_ffn_Ws1"], Ws3=A["L0_ffn_Ws3"], Ws2=A["L0_ffn_Ws2"],
        waT=np.ascontiguousarray(np.asarray(A["L1_attn_w_attn"]).T),
        ba=np.asarray(A["L1_attn_b_attn"]).reshape(1, -1),
        wpT=np.ascontiguousarray(np.asarray(A["L1_attn_w_proj"]).T),
        bp=np.asarray(A["L1_attn_b_proj"]).reshape(1, -1),
        rw=A["L1_ffn_router_w"],
        rb=np.asarray(A["L1_ffn_router_b"]).reshape(1, 16),
        rb16=np.ascontiguousarray(np.tile(np.asarray(A["L1_ffn_router_b"]).reshape(4, 1, 4), (1, 16, 1)).reshape(1, 256)),
        gcols=np.ascontiguousarray(np.asarray(A["L1_ffn_ln_g"]).reshape(4, 4, 8, 128).transpose(3, 0, 1, 2).reshape(128, 128)),
        bcols=np.ascontiguousarray(np.asarray(A["L1_ffn_ln_b"]).reshape(4, 4, 8, 128).transpose(3, 0, 1, 2).reshape(128, 128)),
        b1cols=np.ascontiguousarray(np.asarray(A["L1_ffn_b1"]).reshape(4, 4, 32, 128).transpose(3, 0, 1, 2).reshape(128, 512)),
        w1=A["L1_ffn_w1"], w2=A["L1_ffn_w2"],
        b2l=np.ascontiguousarray(np.asarray(A["L1_ffn_b2"]).transpose(1, 0, 2).reshape(4, 4096)),
    ))
    return {k: np.ascontiguousarray(np.asarray(v, np.float32)) for k, v in shared.items()}


FULL_PARTS = ["l0ffn", "l0attn", "l0ffn", "l1ffn", "l1attn", "l1ffn"]


def run_parts(xfull, shared, parts):
    nc, es, p = build_nc(parts)
    in_maps = []
    for i in range(8):
        m = dict(shared)
        m["xin"] = np.ascontiguousarray(np.asarray(xfull[i], np.float32).T)
        in_maps.append(m)
    res = run_bass_kernel_spmd(nc, in_maps, core_ids=list(range(8)))
    es.close()
    return np.ascontiguousarray(np.stack([np.asarray(r["out"], np.float32).T for r in res.results], 0))


def kernel(**inputs):
    A = {k: np.asarray(v) for k, v in inputs.items()}
    shared = host_inputs(A)
    return run_parts(np.asarray(A["L0_attn_x"], np.float32), shared, FULL_PARTS)
